# version 03
# baseline (speedup 1.0000x reference)
import numpy as np
import ml_dtypes
from contextlib import ExitStack

import concourse.bass as bass
import concourse.mybir as mybir
from concourse.bass_utils import run_bass_kernel_spmd

F32 = mybir.dt.float32
BF16 = mybir.dt.bfloat16
I32 = mybir.dt.int32
AF = mybir.ActivationFunctionType
ALU = mybir.AluOpType
AX = mybir.AxisListType

D = 4096
HD = 128
NKVH = 8
THETA = 10000.0
SM_SCALE = HD ** -0.5
EPS = 1e-5
LORA_SCALE = 1.0
NT_OWN = 8
NT_SPAN = 16
SEQ = 4096
B = 2


class Sched:
    ENGS = ("g", "t", "v", "s", "y")

    def __init__(self):
        self.q = {e: [] for e in self.ENGS}
        self.c = {"PE": 0, "VEC": 0, "SCA": 0}
        self.waited = {}

    def op(self, eng, fn, inc=None, amt=1):
        if inc is not None:
            self.c[inc] = self.c.get(inc, 0) + amt
        self.q[eng].append((fn, inc, amt))
        return self.c[inc] if inc else None

    def wait(self, eng, sem, val):
        if val <= 0:
            return
        key = (eng, sem)
        if self.waited.get(key, 0) >= val:
            return
        self.waited[key] = val
        self.q[eng].append(("W", sem, val))

    def hop(self, eng):
        sem = {"t": "PE", "v": "VEC", "s": "SCA"}[eng]
        self.wait(eng, sem, self.c[sem])

    def barrier(self):
        snap = dict(self.c)
        for e in self.ENGS:
            for sm in snap:
                self.wait(e, sm, snap[sm])

    def replay(self, eng, handle, sems):
        for item in self.q[eng]:
            if item[0] == "W":
                handle.wait_ge(sems[item[1]], item[2])
            else:
                fn, inc, amt = item
                ins = fn(handle)
                if inc is not None:
                    ins.then_inc(sems[inc], amt)


def build(taps=()):
    nc = bass.Bass("TRN2", target_bir_lowering=False)

    xspan = nc.dram_tensor("xspan", [NT_SPAN * 128, D], F32, kind="ExternalInput")
    wqkv = nc.dram_tensor("wqkv", [D, 6144], F32, kind="ExternalInput")
    wo_d = nc.dram_tensor("wo", [D, D], F32, kind="ExternalInput")
    wft = nc.dram_tensor("wft", [D, D], F32, kind="ExternalInput")
    acat_d = nc.dram_tensor("acat", [D, 48], F32, kind="ExternalInput")
    bcat_d = nc.dram_tensor("bcatx", [49, D], F32, kind="ExternalInput")
    lsi_d = nc.dram_tensor("lsi", [1, 4], I32, kind="ExternalInput")
    cost_d = nc.dram_tensor("cost", [128, 2048], F32, kind="ExternalInput")
    sint_d = nc.dram_tensor("sint", [128, 2048], F32, kind="ExternalInput")
    padc_d = nc.dram_tensor("padc", [NT_OWN, 512], F32, kind="ExternalInput")
    ident_d = nc.dram_tensor("ident", [128, 128], F32, kind="ExternalInput")
    maskc_d = nc.dram_tensor("maskc", [128, 512], F32, kind="ExternalInput")
    masks_d = nc.dram_tensor("masks", [128, 512], F32, kind="ExternalInput")
    a2v_d = nc.dram_tensor("a2v", [49, 2], F32, kind="ExternalInput")
    bq_d = nc.dram_tensor("bq", [1, 1], F32, kind="ExternalInput")
    ones_d = nc.dram_tensor("onesd", [1, 1024], F32, kind="ExternalInput")
    out_d = nc.dram_tensor("out", [NT_OWN * 128, D], F32, kind="ExternalOutput")

    qt_scr = nc.dram_tensor("qt_scr", [D, NT_OWN * 128], BF16,
                            kind="ExternalOutput" if "qt" in taps else "Internal")
    x2_scr = nc.dram_tensor("x2_scr", [NT_OWN * 128, D], F32,
                            kind="ExternalOutput" if "x2" in taps else "Internal")

    es = ExitStack()
    sb = lambda n, s, d: es.enter_context(nc.sbuf_tensor(n, s, d))
    ps = lambda n, s, d=F32: es.enter_context(nc.psum_tensor(n, s, d))

    bigT = sb("bigT", [128, 32768], BF16)
    kTb = sb("kTb", [128, 16384], BF16)
    Vb = sb("Vb", [128, 16384], BF16)
    arA0 = sb("arA0", [128, 4096], F32)
    arA1 = sb("arA1", [128, 4096], F32)
    arB = sb("arB", [128, 4096], F32)
    wb = [sb("wb0", [128, 4096], BF16), sb("wb1", [128, 4096], BF16)]
    xnb = sb("xnb", [128, D], BF16)
    identb = sb("identb", [128, 128], BF16)
    maskcb = sb("maskcb", [128, 512], BF16)
    masksb = sb("masksb", [128, 512], BF16)
    onescol = sb("onescol", [128, 1], BF16)
    onesrow = sb("onesrow", [1, 128], F32)
    stats = sb("stats", [128, 16], F32)
    lsib = sb("lsib", [1, 4], I32)
    lsif = sb("lsif", [1, 8], F32)
    rsc = sb("rsc", [1, 8], F32)
    a2vb = sb("a2vb", [49, 2], F32)
    w49 = sb("w49", [49, 1], F32)
    eqv = sb("eqv", [49, 1], F32)
    bqb = sb("bqb", [1, 1], F32)
    lnh = sb("lnh", [128, 224], F32) if "lnh" in taps else None

    rsb = arA1[0:1, 1024:1536]
    rsb2 = arA1[0:1, 1536:2048]
    xnbf = xnb.bitcast(F32)
    padtile = xnbf[0:1, 0:1024]
    arA0b = arA0.bitcast(BF16)
    arA1b = arA1.bitcast(BF16)
    arBb = arB.bitcast(BF16)
    xst = [arA0b[:, 0:4096], arA1b[:, 0:4096]]
    qrope = arA0b[:, 6144:8192]
    qstage = arA1b[:, 6144:8192]
    costb = arB[:, 0:2048]
    sintb = arB[:, 2048:4096]
    ptb = [arA0b[:, 0:512], arA0b[:, 512:1024], arA0b[:, 1024:1536]]
    recipb = [arA1[:, 0:512], arA1[:, 512:1024]]
    qtt = [arBb[:, 0:4096], arBb[:, 4096:8192]]
    colst = [arB[:, 0:2048], arB[:, 2048:4096]]
    ostage = [arA0[:, 0:2048], arA1[:, 0:2048]]
    acat_off = 4096
    zTs_off = 6144
    zbuf_off = 7168

    pA = ps("pA", [128, 2048])
    pB = ps("pB", [128, 2048])
    pAb = pA.bitcast(BF16)
    pBb = pB.bitcast(BF16)

    S = Sched()
    free = {}

    def need(eng, tok):
        if tok in free:
            sem, val = free[tok]
            S.wait(eng, sem, val)

    def release(tok, sem, val):
        free[tok] = (sem, val)

    def dma(eng_q, out_ap, in_ap, sem):
        return S.op(eng_q, lambda E, o=out_ap, i=in_ap: E.dma_start(o, i), inc=sem, amt=16)

    def tap(name, region, shape, dt):
        if name not in taps:
            return
        od = nc.dram_tensor("tap_" + name, shape, dt, kind="ExternalOutput")
        S.barrier()
        dma("y", od[:, :], region, "TP")
        S.barrier()

    dma("g", identb[:, :], ident_d[:, :], "CST")
    dma("g", maskcb[:, :], maskc_d[:, :], "CST")
    dma("g", masksb[:, :], masks_d[:, :], "CST")
    dma("g", costb, cost_d[:, :], "CST")
    dma("g", sintb, sint_d[:, :], "CST")
    dma("g", lsib[:, :], lsi_d[:, :], "CST")
    dma("g", a2vb[:, :], a2v_d[:, :], "CST")
    gd_setup = dma("g", bqb[:, :], bq_d[:, :], "CST")

    S.op("v", lambda E: E.memset(onescol[:, :], 1.0), inc="VEC")
    S.op("v", lambda E: E.memset(onesrow[:, :], 1.0), inc="VEC")
    S.wait("v", "CST", gd_setup)
    S.op("v", lambda E: E.tensor_copy(lsif[:, 0:4], lsib[:, :]), inc="VEC"); S.hop("v")
    S.op("v", lambda E: E.tensor_scalar(lsif[:, 4:8], lsif[:, 0:4], bqb[0:1, 0:1], None, ALU.is_gt), inc="VEC"); S.hop("v")
    c0, c1, c2, c3 = (lsif[:, 4 + i:5 + i] for i in range(4))
    S.op("v", lambda E: E.tensor_tensor(rsc[:, 0:1], c0, c1, ALU.mult), inc="VEC"); S.hop("v")
    S.op("v", lambda E: E.tensor_tensor(rsc[:, 1:2], rsc[:, 0:1], c1, ALU.add), inc="VEC"); S.hop("v")
    S.op("v", lambda E: E.tensor_scalar(rsc[:, 2:3], rsc[:, 1:2], -1.0, 2.0, ALU.mult, ALU.add), inc="VEC"); S.hop("v")
    S.op("v", lambda E: E.tensor_scalar(rsc[:, 3:4], c3, -1.0, 4.0, ALU.mult, ALU.add), inc="VEC"); S.hop("v")
    S.op("v", lambda E: E.tensor_tensor(rsc[:, 4:5], rsc[:, 2:3], rsc[:, 3:4], ALU.subtract), inc="VEC"); S.hop("v")
    S.op("v", lambda E: E.tensor_tensor(rsc[:, 5:6], c2, rsc[:, 4:5], ALU.mult), inc="VEC"); S.hop("v")
    vr = S.op("v", lambda E: E.tensor_tensor(rsc[:, 6:7], rsc[:, 5:6], rsc[:, 3:4], ALU.add), inc="VEC"); S.hop("v")
    S.wait("t", "VEC", vr)
    p_r = S.op("t", lambda E: E.matmul(pB[0:49, 0:1], onesrow[0:1, 0:49], rsc[:, 6:7],
                                       start=True, stop=True, skip_group_check=True), inc="PE")
    S.wait("v", "PE", p_r)
    S.op("v", lambda E: E.tensor_tensor(eqv[:, :], pB[0:49, 0:1], a2vb[:, 0:1], ALU.is_equal), inc="VEC"); S.hop("v")
    S.op("v", lambda E: E.tensor_scalar(eqv[:, :], eqv[:, :], LORA_SCALE, None, ALU.mult), inc="VEC"); S.hop("v")
    vw49 = S.op("v", lambda E: E.tensor_tensor(w49[:, :], eqv[:, :], a2vb[:, 1:2], ALU.add), inc="VEC"); S.hop("v")

    xnT_base = lambda dm, t: (dm * 8 + t) * 128
    ln_state = {"ctr": 0, "last_tr": 0}

    def ln_tile(par, ti, xsem, xval):
        st = stats[:, (ln_state["ctr"] % 2) * 8:(ln_state["ctr"] % 2) * 8 + 8]
        ln_state["ctr"] += 1
        xb = xst[par]
        S.wait("v", xsem, xval)
        S.op("v", lambda E, xb=xb, st=st: E.reduce_sum(st[:, 0:1], xb, AX.X), inc="VEC")
        S.wait("v", "PE", ln_state["last_tr"])
        S.op("v", lambda E, xb=xb, st=st: E.scalar_tensor_tensor(
            xnb[:, :], xb, 1.0, xb, ALU.mult, ALU.mult, accum_out=st[:, 1:2]), inc="VEC"); S.hop("v")
        S.op("v", lambda E, st=st: E.tensor_scalar(st[:, 2:3], st[:, 1:2], 1.0 / D, EPS, ALU.mult, ALU.add), inc="VEC")
        S.op("v", lambda E, st=st: E.tensor_scalar(st[:, 3:4], st[:, 0:1], 1.0 / D, None, ALU.mult), inc="VEC"); S.hop("v")
        S.op("v", lambda E, st=st: E.tensor_tensor(st[:, 4:5], st[:, 3:4], st[:, 3:4], ALU.mult), inc="VEC"); S.hop("v")
        vv = S.op("v", lambda E, st=st: E.tensor_tensor(st[:, 5:6], st[:, 2:3], st[:, 4:5], ALU.subtract), inc="VEC"); S.hop("v")
        S.wait("s", "VEC", vv)
        sd = S.op("s", lambda E, st=st: E.activation(st[:, 6:7], st[:, 5:6], AF.Sqrt), inc="SCA")
        S.wait("v", "SCA", sd)
        S.op("v", lambda E, st=st: E.reciprocal(st[:, 7:8], st[:, 6:7]), inc="VEC"); S.hop("v")
        nrm = S.op("v", lambda E, xb=xb, st=st: E.tensor_scalar(
            xnb[:, :], xb, st[:, 3:4], st[:, 7:8], ALU.subtract, ALU.mult), inc="VEC")
        if lnh is not None:
            k8 = ln_state["ctr"] - 1
            S.op("v", lambda E, st=st, k8=k8: E.tensor_copy(lnh[:, k8 * 8:k8 * 8 + 8], st), inc="VEC")
        release(("xst", par), "VEC", nrm)
        for gblk in range(4):
            tb = pAb[:, (gblk % 2) * 1024:(gblk % 2) * 1024 + 1024]
            need("t", ("trbank", gblk % 2))
            S.wait("t", "VEC", nrm)
            pe_last = None
            for j in range(8):
                dm = gblk * 8 + j
                pe_last = S.op("t", lambda E, dm=dm, tb=tb, j=j: E.transpose(
                    tb[:, j * 128:(j + 1) * 128], xnb[:, dm * 128:(dm + 1) * 128], identb[:, :]), inc="PE")
            S.wait("v", "PE", pe_last)
            base0 = xnT_base(gblk * 8, ti)
            ev = S.op("v", lambda E, tb=tb, base0=base0: E.tensor_copy(
                bass.AP(bigT, base0, [[32768, 128], [1024, 8], [1, 128]]), tb[:, 0:1024]), inc="VEC")
            release(("trbank", gblk % 2), "VEC", ev)
        ln_state["last_tr"] = S.c["PE"]

    def stream_weights(dram, col0):
        vals = []
        for wh in range(2):
            need("g", ("wb", wh))
            src = bass.AP(dram, (wh * 2048) * dram.shape[1] + col0,
                          [[dram.shape[1], 128], [128 * dram.shape[1], 16], [1, 256]])
            dst = bass.AP(wb[wh], 0, [[4096, 128], [256, 16], [1, 256]])
            vals.append(dma("g", dst, src, "WB%d" % wh))
        return vals

    def proj_og(par, gd_after, no_stop=False):
        pX = pA if par == 0 else pB
        pe_last = None
        for wh in range(2):
            S.wait("t", "WB%d" % wh, gd_after[wh])
            for dm in range(16):
                dmg = wh * 16 + dm
                for t in range(8):
                    st_ = (wh == 0 and dm == 0 and t % 2 == 0)
                    sp_ = (wh == 1 and dm == 15) and not no_stop
                    pe_last = S.op("t", lambda E, pX=pX, t=t, dmg=dmg, dm=dm, wh=wh, st_=st_, sp_=sp_:
                                   E.matmul(pX[:, t * 256:(t + 1) * 256],
                                            bigT[:, xnT_base(dmg, t):xnT_base(dmg, t) + 128],
                                            wb[wh][:, dm * 256:(dm + 1) * 256],
                                            start=st_, stop=sp_, skip_group_check=True), inc="PE")
        return pe_last

    for h in range(2):
        for i in range(8):
            lv = 8 * h + i
            par = i % 2
            need("g", ("xst", par))
            gd_x = dma("g", xst[par], xspan[lv * 128:(lv + 1) * 128, :], "XST%d" % par)
            ln_tile(par, i, "XST%d" % par, gd_x)
        S.barrier()
        if h == 0:
            tap("xnT0", bigT[:, :], [128, 32768], BF16)

        og_list = list(range(24)) if h == 1 else list(range(16, 24))
        pend = None

        def emit_tr_evac(og, par, kind, rope_done):
            S.wait("t", "VEC", rope_done)
            pXb_ = pAb if par == 0 else pBb
            pe_tr = None
            for t in range(8):
                for hh in range(2):
                    j = t * 2 + hh
                    pe_tr = S.op("t", lambda E, j=j, pXb_=pXb_: E.transpose(
                        pXb_[:, j * 128:(j + 1) * 128],
                        qrope[:, j * 128:(j + 1) * 128], identb[:, :]), inc="PE")
            release(("qrope",), "PE", pe_tr)
            S.wait("v", "PE", pe_tr)
            if kind == "q":
                need("v", ("qstage",))
                ev = S.op("v", lambda E, pXb_=pXb_: E.tensor_copy(qstage[:, 0:2048], pXb_[:, 0:2048]), inc="VEC")
                release(("pogpar", par), "VEC", ev)
                S.wait("y", "VEC", ev)
                sy = None
                for hh in range(2):
                    dst = bass.AP(qt_scr, (og * 256 + hh * 128) * 1024, [[1024, 128], [128, 8], [1, 128]])
                    src = bass.AP(arA1.bitcast(BF16), 6144 + hh * 128, [[8192, 128], [256, 8], [1, 128]])
                    sy = dma("y", dst, src, "QS")
                release(("qstage",), "QS", sy)
            else:
                ko = og - 16
                dst = bass.AP(kTb, (ko * 2) * 2048 + (8 * h) * 128,
                              [[16384, 128], [2048, 2], [128, 8], [1, 128]])
                srcb = bass.AP(pXb_, 0, [[4096, 128], [128, 2], [256, 8], [1, 128]])
                ev = S.op("v", lambda E, dst=dst, srcb=srcb: E.tensor_copy(dst, srcb), inc="VEC")
                release(("pogpar", par), "VEC", ev)

        for og in og_list:
            par = og % 2
            gd_after = stream_weights(wqkv, og * 256)
            need("t", ("pogpar", par))
            pe_mm = proj_og(par, gd_after)
            release(("wb", 0), "PE", pe_mm)
            release(("wb", 1), "PE", pe_mm)
            if pend is not None:
                emit_tr_evac(*pend)
                pend = None
            pX = pA if par == 0 else pB
            kind = "q" if og < 16 else ("k" if og < 20 else "v")
            if kind in ("q", "k"):
                S.wait("v", "PE", pe_mm)
                need("v", ("qrope",))
                q1 = bass.AP(pX, 0, [[2048, 128], [256, 8], [128, 2], [1, 64]])
                q2 = bass.AP(pX, 64, [[2048, 128], [256, 8], [128, 2], [1, 64]])
                cosv = bass.AP(arB, (8 * h) * 128, [[4096, 128], [128, 8], [64, 2], [1, 64]])
                sinv = bass.AP(arB, 2048 + (8 * h) * 128, [[4096, 128], [128, 8], [64, 2], [1, 64]])
                t1v = bass.AP(arA0, 2048, [[4096, 128], [128, 8], [64, 2], [1, 64]])
                t2v = bass.AP(arA1, 2048, [[4096, 128], [128, 8], [64, 2], [1, 64]])
                qr1 = bass.AP(arA0.bitcast(BF16), 6144, [[8192, 128], [256, 8], [128, 2], [1, 64]])
                qr2 = bass.AP(arA0.bitcast(BF16), 6144 + 64, [[8192, 128], [256, 8], [128, 2], [1, 64]])
                S.op("v", lambda E, q1=q1, cosv=cosv, t1v=t1v: E.tensor_tensor(t1v, q1, cosv, ALU.mult), inc="VEC")
                S.op("v", lambda E, q2=q2, sinv=sinv, t2v=t2v: E.tensor_tensor(t2v, q2, sinv, ALU.mult), inc="VEC")
                S.op("v", lambda E, t1v=t1v, t2v=t2v, qr1=qr1: E.tensor_tensor(qr1, t1v, t2v, ALU.subtract), inc="VEC")
                S.op("v", lambda E, q1=q1, sinv=sinv, t1v=t1v: E.tensor_tensor(t1v, q1, sinv, ALU.mult), inc="VEC")
                S.op("v", lambda E, q2=q2, cosv=cosv, t2v=t2v: E.tensor_tensor(t2v, q2, cosv, ALU.mult), inc="VEC")
                rope_done = S.op("v", lambda E, t1v=t1v, t2v=t2v, qr2=qr2: E.tensor_tensor(qr2, t1v, t2v, ALU.add), inc="VEC")
                pend = (og, par, kind, rope_done)
            else:
                vo = og - 20
                S.wait("s", "PE", pe_mm)
                dst = bass.AP(Vb, (8 * h) * 1024 + vo * 256, [[16384, 128], [1024, 8], [1, 256]])
                src = bass.AP(pX, 0, [[2048, 128], [256, 8], [1, 256]])
                ev = S.op("s", lambda E, dst=dst, src=src: E.copy(dst, src), inc="SCA")
                release(("pogpar", par), "SCA", ev)
        if pend is not None:
            emit_tr_evac(*pend)
            pend = None
        S.barrier()

    tap("kT", kTb[:, :], [128, 16384], BF16)
    tap("V", Vb[:, :], [128, 16384], BF16)

    for t in range(NT_OWN):
        qsem = "QT%d" % (t % 2)
        need("y", ("qtt", t % 2))
        src = bass.AP(qt_scr, t * 128, [[1024, 128], [128 * 1024, 32], [1, 128]])
        dst = bass.AP(arB.bitcast(BF16), (t % 2) * 4096, [[8192, 128], [128, 32], [1, 128]])
        sy_qt = dma("y", dst, src, qsem)
        need("y", ("padt", t % 2))
        pd = dma("y", xnbf[0:1, (t % 2) * 512:(t % 2) * 512 + 512], padc_d[t:t + 1, :], qsem)
        S.wait("t", qsem, sy_qt)
        for hk in range(NKVH):
            pair = t * NKVH + hk
            par = pair % 2
            psT = [pA[:, 0:512], pA[:, 512:1024]]
            psums = [pA[0:1, 1024:1536], pA[0:1, 1536:2048]]
            pav = [pB[:, 0:512], pB[:, 512:1024]]
            pbc = pB[:, 1024:1536]
            qt_rhs = qtt[t % 2][:, (4 * hk) * 128:(4 * hk) * 128 + 512]
            exp_of = {}
            for c in range(9):
                cpar = c % 2
                lvk = t + c
                if c >= 2:
                    S.wait("t", "SCA", exp_of[c - 2])
                else:
                    need("t", ("psT", cpar))
                mm = S.op("t", lambda E, cpar=cpar, hk=hk, lvk=lvk, qt_rhs=qt_rhs, psT=psT:
                          E.matmul(psT[cpar], kTb[:, hk * 2048 + lvk * 128:hk * 2048 + lvk * 128 + 128],
                                   qt_rhs, start=True, stop=True, skip_group_check=True), inc="PE")
                S.wait("s", "PE", mm)
                need("s", ("ptb", c % 3))
                ex = S.op("s", lambda E, c=c, cpar=cpar, psT=psT: E.activation(
                    ptb[c % 3], psT[cpar], AF.Exp, scale=SM_SCALE), inc="SCA")
                exp_of[c] = ex
                gate_sem, gate_val = "SCA", ex
                if c == 0 or c == 8:
                    S.wait("v", "SCA", ex)
                    mk = masksb if c == 0 else maskcb
                    mv = S.op("v", lambda E, c=c, mk=mk: E.tensor_tensor(ptb[c % 3], ptb[c % 3], mk[:, :], ALU.mult), inc="VEC")
                    gate_sem, gate_val = "VEC", mv
                S.wait("t", gate_sem, gate_val)
                if c == 0:
                    need("t", ("psums", par))
                    need("t", ("pav", par))
                S.op("t", lambda E, c=c, par=par, psums=psums: E.matmul(
                    psums[par], onescol[:, :], ptb[c % 3], start=(c == 0), stop=(c == 8), skip_group_check=True), inc="PE")
                av = S.op("t", lambda E, c=c, par=par, hk=hk, lvk=lvk, pav=pav: E.matmul(
                    pav[par], Vb[:, lvk * 1024 + hk * 128:lvk * 1024 + hk * 128 + 128],
                    ptb[c % 3], start=(c == 0), stop=(c == 8), skip_group_check=True), inc="PE")
                release(("ptb", c % 3), "PE", av)
                if c >= 7:
                    release(("psT", c % 2), "SCA", exp_of[c])
            pe_pair_last = S.c["PE"]
            S.wait("v", "PE", pe_pair_last)
            S.wait("v", qsem, pd)
            S.op("v", lambda E, par=par, psums=psums, t=t: E.tensor_tensor(
                rsb, psums[par], xnbf[0:1, (t % 2) * 512:(t % 2) * 512 + 512], ALU.subtract), inc="VEC"); S.hop("v")
            rr = S.op("v", lambda E: E.reciprocal(rsb2, rsb), inc="VEC"); S.hop("v")
            release(("psums", par), "VEC", rr)
            S.wait("t", "VEC", rr)
            need("t", ("pbc",))
            bc = S.op("t", lambda E, pbc=pbc: E.matmul(pbc, onesrow[0:1, :], rsb2,
                                                       start=True, stop=True, skip_group_check=True), inc="PE")
            S.wait("s", "PE", bc)
            need("s", ("recipb", par))
            cp = S.op("s", lambda E, par=par, pbc=pbc: E.copy(recipb[par], pbc), inc="SCA")
            release(("pbc",), "SCA", cp)
            S.wait("v", "SCA", cp)
            dstb = bass.AP(bigT, (4 * hk * 8 + t) * 128, [[32768, 128], [1024, 4], [1, 128]])
            ev = S.op("v", lambda E, par=par, dstb=dstb, pav=pav: E.tensor_tensor(
                dstb, pav[par], recipb[par], ALU.mult), inc="VEC")
            release(("pav", par), "VEC", ev)
            release(("recipb", par), "VEC", ev)
            release(("padt", t % 2), "VEC", rr)
        release(("qtt", t % 2), "PE", S.c["PE"])
    S.barrier()
    tap("attnT", bigT[:, :], [128, 32768], BF16)

    for og in range(16):
        par = og % 2
        gd_after = stream_weights(wo_d, og * 256)
        need("y", ("colst", par))
        csrc = bass.AP(xspan, (8 * 128) * D + og * 256, [[D, 128], [128 * D, 8], [1, 256]])
        cdst = bass.AP(arB, par * 2048, [[4096, 128], [256, 8], [1, 256]])
        sy_col = dma("y", cdst, csrc, "CL%d" % par)
        need("t", ("pogpar", par))
        pe_mm = proj_og(par, gd_after)
        release(("wb", 0), "PE", pe_mm)
        release(("wb", 1), "PE", pe_mm)
        pX = pA if par == 0 else pB
        S.wait("v", "PE", pe_mm)
        S.wait("v", "CL%d" % par, sy_col)
        need("v", ("ostage", par))
        ev = S.op("v", lambda E, pX=pX, par=par: E.tensor_tensor(ostage[par], pX[:, 0:2048], colst[par], ALU.add), inc="VEC")
        release(("pogpar", par), "VEC", ev)
        release(("colst", par), "VEC", ev)
        S.wait("y", "VEC", ev)
        dst = bass.AP(x2_scr, og * 256, [[D, 128], [128 * D, 8], [1, 256]])
        src = bass.AP(arA0 if par == 0 else arA1, 0, [[4096, 128], [256, 8], [1, 256]])
        syx2 = dma("y", dst, src, "OS%d" % par)
        release(("ostage", par), "OS%d" % par, syx2)
    S.barrier()

    adst = bass.AP(kTb, acat_off, [[16384, 128], [48, 32], [1, 48]])
    asrc = bass.AP(acat_d, 0, [[48, 128], [128 * 48, 32], [1, 48]])
    gd_ac = dma("g", adst, asrc, "CST")
    gd_bc = dma("g", kTb[0:49, 0:4096], bcat_d[:, :], "CST")
    dma("g", kTb[48:49, zTs_off:zTs_off + 1024], ones_d[:, :], "CST")

    for t in range(NT_OWN):
        par = t % 2
        need("g", ("xst", par))
        gd_x = dma("g", xst[par], x2_scr[t * 128:(t + 1) * 128, :], "XST%d" % par)
        ln_tile(par, t, "XST%d" % par, gd_x)
        S.wait("t", "CST", gd_ac)
        need("t", ("pz",))
        pz = pB[:, 1536:1584]
        pe_z = None
        for dm in range(32):
            pe_z = S.op("t", lambda E, dm=dm, t=t, pz=pz: E.matmul(
                pz, bigT[:, xnT_base(dm, t):xnT_base(dm, t) + 128],
                kTb[:, acat_off + dm * 48:acat_off + dm * 48 + 48],
                start=(dm == 0), stop=(dm == 31), skip_group_check=True), inc="PE")
        S.wait("s", "PE", pe_z)
        zb = S.op("s", lambda E, pz=pz: E.copy(kTb[:, zbuf_off:zbuf_off + 48], pz), inc="SCA")
        S.wait("t", "SCA", zb)
        pzT = pBb[:, 3200:3328]
        tz = S.op("t", lambda E, pzT=pzT: E.transpose(pzT[0:48, :], kTb[:, zbuf_off:zbuf_off + 48], identb[:, :]), inc="PE")
        S.wait("v", "PE", tz)
        S.wait("v", "VEC", vw49)
        vz = S.op("v", lambda E, t=t, pzT=pzT: E.tensor_scalar(
            kTb[0:48, zTs_off + t * 128:zTs_off + (t + 1) * 128], pzT[0:48, :], w49[0:48, 0:1], None, ALU.mult), inc="VEC")
        release(("pz",), "VEC", vz)
    ln2_vec = S.c["VEC"]
    S.barrier()
    tap("xn2T", bigT[:, :], [128, 32768], BF16)
    tap("zTs", kTb[0:49, zTs_off:zTs_off + 1024], [49, 1024], BF16)
    tap("w49", w49[:, :], [49, 1], F32)
    tap("lsig", lsif[:, :], [1, 8], F32)
    tap("rscg", rsc[:, :], [1, 8], F32)
    tap("eqvg", eqv[:, :], [49, 1], F32)

    for og in range(16):
        par = og % 2
        gd_after = stream_weights(wft, og * 256)
        need("y", ("colst", par))
        csrc = bass.AP(x2_scr, og * 256, [[D, 128], [128 * D, 8], [1, 256]])
        cdst = bass.AP(arB, par * 2048, [[4096, 128], [256, 8], [1, 256]])
        sy_col = dma("y", cdst, csrc, "CL%d" % par)
        need("t", ("pogpar", par))
        pe_mm = proj_og(par, gd_after, no_stop=True)
        pX = pA if par == 0 else pB
        pe_l = None
        for t in range(8):
            pe_l = S.op("t", lambda E, t=t, og=og, pX=pX: E.matmul(
                pX[:, t * 256:(t + 1) * 256], kTb[0:49, zTs_off + t * 128:zTs_off + (t + 1) * 128],
                kTb[0:49, og * 256:(og + 1) * 256],
                start=False, stop=True, skip_group_check=True), inc="PE")
        release(("wb", 0), "PE", pe_mm)
        release(("wb", 1), "PE", pe_mm)
        S.wait("v", "PE", pe_l)
        S.wait("v", "CL%d" % par, sy_col)
        need("v", ("ostage", par))
        ev = S.op("v", lambda E, pX=pX, par=par: E.tensor_tensor(ostage[par], pX[:, 0:2048], colst[par], ALU.add), inc="VEC")
        release(("pogpar", par), "VEC", ev)
        release(("colst", par), "VEC", ev)
        S.wait("y", "VEC", ev)
        dst = bass.AP(out_d, og * 256, [[D, 128], [128 * D, 8], [1, 256]])
        src = bass.AP(arA0 if par == 0 else arA1, 0, [[4096, 128], [256, 8], [1, 256]])
        syo = dma("y", dst, src, "OS%d" % par)
        release(("ostage", par), "OS%d" % par, syo)

    if lnh is not None:
        tap("lnh", lnh[:, :], [128, 224], F32)
    S.wait("y", "OS0", S.c.get("OS0", 0))
    S.wait("y", "OS1", S.c.get("OS1", 0))

    with ExitStack() as emit_es:
        sems = {n: emit_es.enter_context(nc.semaphore(n)) for n in sorted(S.c)}
        with nc.Block() as block:
            @block.gpsimd
            def _(g):
                S.replay("g", g, sems)

            @block.tensor
            def _(t):
                S.replay("t", t, sems)

            @block.vector
            def _(v):
                S.replay("v", v, sems)

            @block.scalar
            def _(s):
                S.replay("s", s, sems)

            @block.sync
            def _(y):
                S.replay("y", y, sems)
    es.close()
    return nc


_CACHE = {}


def _host_constants():
    if "consts" in _CACHE:
        return _CACHE["consts"]
    ident = np.eye(128, dtype=np.float32)
    cc = np.arange(128)[:, None]
    rr = np.arange(128)[None, :]
    causal = (cc <= rr).astype(np.float32)
    strict = (cc > rr).astype(np.float32)
    maskc = np.tile(causal, (1, 4)).astype(np.float32)
    masks = np.tile(strict, (1, 4)).astype(np.float32)
    a2v = np.zeros((49, 2), dtype=np.float32)
    a2v[0:48, 0] = (np.arange(48) // 16 + 2).astype(np.float32)
    a2v[48, 0] = -1.0
    a2v[48, 1] = 1.0
    _CACHE["consts"] = (ident, maskc, masks, a2v)
    return _CACHE["consts"]


def prep_in_maps(L0_attn_x, L0_attn_wq, L0_attn_wk, L0_attn_wv, L0_attn_wo,
                 L0_ffn_weight, L0_ffn_bias, L0_ffn_lora_A, L0_ffn_lora_B,
                 L0_ffn_lora_start_indices):
    x = np.ascontiguousarray(np.asarray(L0_attn_x, dtype=np.float32))
    wqkv = np.ascontiguousarray(np.concatenate(
        [np.asarray(L0_attn_wq, np.float32), np.asarray(L0_attn_wk, np.float32),
         np.asarray(L0_attn_wv, np.float32)], axis=1))
    wo = np.ascontiguousarray(np.asarray(L0_attn_wo, np.float32))
    wft = np.ascontiguousarray(np.asarray(L0_ffn_weight, np.float32).T)
    acat = np.ascontiguousarray(np.asarray(L0_ffn_lora_A, np.float32).transpose(2, 0, 1).reshape(D, 48))
    bcatx = np.ascontiguousarray(np.concatenate(
        [np.asarray(L0_ffn_lora_B, np.float32).transpose(0, 2, 1).reshape(48, D),
         np.asarray(L0_ffn_bias, np.float32)[None, :]], axis=0))
    lsi = np.ascontiguousarray(np.asarray(L0_ffn_lora_start_indices).astype(np.int32).reshape(1, 4))
    ident, maskc, masks, a2v = _host_constants()

    inv_freq = THETA ** (-np.arange(64, dtype=np.float64) / 64.0)

    in_maps = []
    for c in range(8):
        b = c // 4
        Q = c % 4
        if Q == 0:
            xs = np.concatenate([np.zeros((1024, D), np.float32), x[b, 0:1024]], axis=0)
        else:
            xs = np.ascontiguousarray(x[b, (Q - 1) * 1024:(Q + 1) * 1024])
        span0 = Q * 1024 - 1024
        pos = np.maximum(span0 + np.arange(2048), 0).astype(np.float64)
        ang = pos[:, None] * inv_freq[None, :]
        cosv = np.cos(ang).astype(np.float32).reshape(16, 128, 64)
        sinv = np.sin(ang).astype(np.float32).reshape(16, 128, 64)
        cost = np.zeros((128, 16, 2, 64), np.float32)
        sint = np.zeros((128, 16, 2, 64), np.float32)
        for hh in range(2):
            cost[:, :, hh, :] = cosv.transpose(1, 0, 2)
            sint[:, :, hh, :] = sinv.transpose(1, 0, 2)
        cost = np.ascontiguousarray(cost.reshape(128, 2048))
        sint = np.ascontiguousarray(sint.reshape(128, 2048))
        padc = np.zeros((8, 512), np.float32)
        if Q == 0:
            rq = np.arange(128, dtype=np.float32)
            for t in range(8):
                padc[t] = np.tile((127.0 - rq) + 128.0 * (7 - t), 4)
        bq = np.full((1, 1), float(b), np.float32)
        onesd = np.ones((1, 1024), np.float32)
        in_maps.append({
            "xspan": xs, "wqkv": wqkv, "wo": wo, "wft": wft, "acat": acat,
            "bcatx": bcatx, "lsi": lsi, "cost": cost, "sint": sint,
            "padc": padc, "ident": ident, "maskc": maskc, "masks": masks,
            "a2v": a2v, "bq": bq, "onesd": onesd,
        })
    return in_maps


def kernel(**inputs):
    in_maps = prep_in_maps(**inputs)
    if "nc" not in _CACHE:
        _CACHE["nc"] = build()
    nc = _CACHE["nc"]
    res = run_bass_kernel_spmd(nc, in_maps, core_ids=list(range(8)))
    out = np.empty((B, SEQ, D), np.float32)
    for c in range(8):
        b = c // 4
        Q = c % 4
        out[b, Q * 1024:(Q + 1) * 1024] = res.results[c]["out"]
    return out


# version 05
# speedup vs baseline: 1.4800x; 1.4800x over previous
import numpy as np
import ml_dtypes
from contextlib import ExitStack

import concourse.bass as bass
import concourse.mybir as mybir
from concourse.bass_utils import run_bass_kernel_spmd

F32 = mybir.dt.float32
BF16 = mybir.dt.bfloat16
I32 = mybir.dt.int32
AF = mybir.ActivationFunctionType
ALU = mybir.AluOpType
AX = mybir.AxisListType

D = 4096
HD = 128
NKVH = 8
THETA = 10000.0
SM_SCALE = HD ** -0.5
EPS = 1e-5
LORA_SCALE = 1.0
NT_OWN = 8
NT_SPAN = 16
SEQ = 4096
B = 2


class Sched:
    ENGS = ("g", "t", "v", "s", "y")

    def __init__(self):
        self.q = {e: [] for e in self.ENGS}
        self.c = {"PE": 0, "VEC": 0, "SCA": 0}
        self.waited = {}

    def op(self, eng, fn, inc=None, amt=1):
        if inc is not None:
            self.c[inc] = self.c.get(inc, 0) + amt
        self.q[eng].append((fn, inc, amt))
        return self.c[inc] if inc else None

    def wait(self, eng, sem, val):
        if val <= 0:
            return
        key = (eng, sem)
        if self.waited.get(key, 0) >= val:
            return
        self.waited[key] = val
        self.q[eng].append(("W", sem, val))

    def hop(self, eng):
        sem = {"t": "PE", "v": "VEC", "s": "SCA"}[eng]
        self.wait(eng, sem, self.c[sem])

    def barrier(self):
        snap = dict(self.c)
        for e in self.ENGS:
            for sm in snap:
                self.wait(e, sm, snap[sm])

    def replay(self, eng, handle, sems):
        for item in self.q[eng]:
            if item[0] == "W":
                handle.wait_ge(sems[item[1]], item[2])
            else:
                fn, inc, amt = item
                ins = fn(handle)
                if inc is not None:
                    ins.then_inc(sems[inc], amt)


def build(taps=()):
    nc = bass.Bass("TRN2", target_bir_lowering=False)

    xspan = nc.dram_tensor("xspan", [NT_SPAN * 128, D], F32, kind="ExternalInput")
    wqkv = nc.dram_tensor("wqkv", [D, 6144], F32, kind="ExternalInput")
    wo_d = nc.dram_tensor("wo", [D, D], F32, kind="ExternalInput")
    wft = nc.dram_tensor("wft", [D, D], F32, kind="ExternalInput")
    acat_d = nc.dram_tensor("acat", [D, 48], F32, kind="ExternalInput")
    bcat_d = nc.dram_tensor("bcatx", [49, D], F32, kind="ExternalInput")
    lsi_d = nc.dram_tensor("lsi", [1, 4], I32, kind="ExternalInput")
    cost_d = nc.dram_tensor("cost", [128, 2048], F32, kind="ExternalInput")
    sint_d = nc.dram_tensor("sint", [128, 2048], F32, kind="ExternalInput")
    padc_d = nc.dram_tensor("padc", [NT_OWN, 512], F32, kind="ExternalInput")
    ident_d = nc.dram_tensor("ident", [128, 128], F32, kind="ExternalInput")
    maskc_d = nc.dram_tensor("maskc", [128, 512], F32, kind="ExternalInput")
    masks_d = nc.dram_tensor("masks", [128, 512], F32, kind="ExternalInput")
    a2v_d = nc.dram_tensor("a2v", [49, 2], F32, kind="ExternalInput")
    bq_d = nc.dram_tensor("bq", [1, 1], F32, kind="ExternalInput")
    ones_d = nc.dram_tensor("onesd", [1, 1024], F32, kind="ExternalInput")
    out_d = nc.dram_tensor("out", [NT_OWN * 128, D], F32, kind="ExternalOutput")

    qt_scr = nc.dram_tensor("qt_scr", [D, NT_OWN * 128], BF16,
                            kind="ExternalOutput" if "qt" in taps else "Internal")
    x2_scr = nc.dram_tensor("x2_scr", [NT_OWN * 128, D], F32,
                            kind="ExternalOutput" if "x2" in taps else "Internal")

    es = ExitStack()
    sb = lambda n, s, d: es.enter_context(nc.sbuf_tensor(n, s, d))
    ps = lambda n, s, d=F32: es.enter_context(nc.psum_tensor(n, s, d))

    bigT = sb("bigT", [128, 32768], BF16)
    kTb = sb("kTb", [128, 16384], BF16)
    Vb = sb("Vb", [128, 16384], BF16)
    arA0 = sb("arA0", [128, 4096], F32)
    arA1 = sb("arA1", [128, 4096], F32)
    arB = sb("arB", [128, 4096], F32)
    wb = [sb("wb0", [128, 4096], BF16), sb("wb1", [128, 4096], BF16)]
    xnb = sb("xnb", [128, D], BF16)
    identb = sb("identb", [128, 128], BF16)
    maskcb = sb("maskcb", [128, 512], BF16)
    masksb = sb("masksb", [128, 512], BF16)
    onescol = sb("onescol", [128, 1], BF16)
    onesrow = sb("onesrow", [1, 128], F32)
    stats = sb("stats", [128, 16], F32)
    lsib = sb("lsib", [1, 4], I32)
    lsif = sb("lsif", [1, 8], F32)
    rsc = sb("rsc", [1, 8], F32)
    a2vb = sb("a2vb", [49, 2], F32)
    w49 = sb("w49", [49, 1], F32)
    eqv = sb("eqv", [49, 1], F32)
    bqb = sb("bqb", [1, 1], F32)
    lnh = sb("lnh", [128, 224], F32) if "lnh" in taps else None

    rsb = arA1[0:1, 1024:1536]
    rsb2 = arA1[0:1, 1536:2048]
    xnbf = xnb.bitcast(F32)
    padtile = xnbf[0:1, 0:1024]
    arA0b = arA0.bitcast(BF16)
    arA1b = arA1.bitcast(BF16)
    arBb = arB.bitcast(BF16)
    xst = [arA0b[:, 0:4096], arA1b[:, 0:4096]]
    qrope = arA0b[:, 6144:8192]
    qstage = arA1b[:, 6144:8192]
    costb = arB[:, 0:2048]
    sintb = arB[:, 2048:4096]
    ptb = [arA0b[:, 0:512], arA0b[:, 512:1024], arA0b[:, 1024:1536]]
    recipb = [arA1[:, 0:512], arA1[:, 512:1024]]
    qtt = [arBb[:, 0:4096], arBb[:, 4096:8192]]
    colst = [arB[:, 0:2048], arB[:, 2048:4096]]
    ostage = [arA0[:, 0:2048], arA1[:, 0:2048]]
    acat_off = 4096
    zTs_off = 6144
    zbuf_off = 7168

    pA = ps("pA", [128, 2048])
    pB = ps("pB", [128, 2048])
    pAb = pA.bitcast(BF16)
    pBb = pB.bitcast(BF16)

    S = Sched()
    free = {}

    def need(eng, tok):
        if tok in free:
            sem, val = free[tok]
            S.wait(eng, sem, val)

    def release(tok, sem, val):
        free[tok] = (sem, val)

    def dma(eng_q, out_ap, in_ap, sem):
        return S.op(eng_q, lambda E, o=out_ap, i=in_ap: E.dma_start(o, i), inc=sem, amt=16)

    def tap(name, region, shape, dt):
        if name not in taps:
            return
        od = nc.dram_tensor("tap_" + name, shape, dt, kind="ExternalOutput")
        S.barrier()
        dma("y", od[:, :], region, "TP")
        S.barrier()

    dma("g", identb[:, :], ident_d[:, :], "CST")
    dma("g", maskcb[:, :], maskc_d[:, :], "CST")
    dma("g", masksb[:, :], masks_d[:, :], "CST")
    dma("g", costb, cost_d[:, :], "CST")
    dma("g", sintb, sint_d[:, :], "CST")
    dma("g", lsib[:, :], lsi_d[:, :], "CST")
    dma("g", a2vb[:, :], a2v_d[:, :], "CST")
    gd_setup = dma("g", bqb[:, :], bq_d[:, :], "CST")

    S.op("v", lambda E: E.memset(onescol[:, :], 1.0), inc="VEC")
    S.op("v", lambda E: E.memset(onesrow[:, :], 1.0), inc="VEC")
    S.wait("v", "CST", gd_setup)
    S.op("v", lambda E: E.tensor_copy(lsif[:, 0:4], lsib[:, :]), inc="VEC"); S.hop("v")
    S.op("v", lambda E: E.tensor_scalar(lsif[:, 4:8], lsif[:, 0:4], bqb[0:1, 0:1], None, ALU.is_gt), inc="VEC"); S.hop("v")
    c0, c1, c2, c3 = (lsif[:, 4 + i:5 + i] for i in range(4))
    S.op("v", lambda E: E.tensor_tensor(rsc[:, 0:1], c0, c1, ALU.mult), inc="VEC"); S.hop("v")
    S.op("v", lambda E: E.tensor_tensor(rsc[:, 1:2], rsc[:, 0:1], c1, ALU.add), inc="VEC"); S.hop("v")
    S.op("v", lambda E: E.tensor_scalar(rsc[:, 2:3], rsc[:, 1:2], -1.0, 2.0, ALU.mult, ALU.add), inc="VEC"); S.hop("v")
    S.op("v", lambda E: E.tensor_scalar(rsc[:, 3:4], c3, -1.0, 4.0, ALU.mult, ALU.add), inc="VEC"); S.hop("v")
    S.op("v", lambda E: E.tensor_tensor(rsc[:, 4:5], rsc[:, 2:3], rsc[:, 3:4], ALU.subtract), inc="VEC"); S.hop("v")
    S.op("v", lambda E: E.tensor_tensor(rsc[:, 5:6], c2, rsc[:, 4:5], ALU.mult), inc="VEC"); S.hop("v")
    vr = S.op("v", lambda E: E.tensor_tensor(rsc[:, 6:7], rsc[:, 5:6], rsc[:, 3:4], ALU.add), inc="VEC"); S.hop("v")
    S.wait("t", "VEC", vr)
    p_r = S.op("t", lambda E: E.matmul(pB[0:49, 0:1], onesrow[0:1, 0:49], rsc[:, 6:7],
                                       start=True, stop=True, skip_group_check=True), inc="PE")
    S.wait("v", "PE", p_r)
    S.op("v", lambda E: E.tensor_tensor(eqv[:, :], pB[0:49, 0:1], a2vb[:, 0:1], ALU.is_equal), inc="VEC"); S.hop("v")
    S.op("v", lambda E: E.tensor_scalar(eqv[:, :], eqv[:, :], LORA_SCALE, None, ALU.mult), inc="VEC"); S.hop("v")
    vw49 = S.op("v", lambda E: E.tensor_tensor(w49[:, :], eqv[:, :], a2vb[:, 1:2], ALU.add), inc="VEC"); S.hop("v")

    xnT_base = lambda dm, t: (dm * 8 + t) * 128
    ln_state = {"ctr": 0, "last_tr": 0}

    def ln_tile(par, ti, xsem, xval):
        st = stats[:, (ln_state["ctr"] % 2) * 8:(ln_state["ctr"] % 2) * 8 + 8]
        ln_state["ctr"] += 1
        xb = xst[par]
        S.wait("v", xsem, xval)
        S.op("v", lambda E, xb=xb, st=st: E.reduce_sum(st[:, 0:1], xb, AX.X), inc="VEC")
        S.wait("v", "PE", ln_state["last_tr"])
        S.op("v", lambda E, xb=xb, st=st: E.scalar_tensor_tensor(
            xnb[:, :], xb, 1.0, xb, ALU.mult, ALU.mult, accum_out=st[:, 1:2]), inc="VEC"); S.hop("v")
        S.op("v", lambda E, st=st: E.tensor_scalar(st[:, 2:3], st[:, 1:2], 1.0 / D, EPS, ALU.mult, ALU.add), inc="VEC")
        S.op("v", lambda E, st=st: E.tensor_scalar(st[:, 3:4], st[:, 0:1], 1.0 / D, None, ALU.mult), inc="VEC"); S.hop("v")
        S.op("v", lambda E, st=st: E.tensor_tensor(st[:, 4:5], st[:, 3:4], st[:, 3:4], ALU.mult), inc="VEC"); S.hop("v")
        vv = S.op("v", lambda E, st=st: E.tensor_tensor(st[:, 5:6], st[:, 2:3], st[:, 4:5], ALU.subtract), inc="VEC"); S.hop("v")
        S.wait("s", "VEC", vv)
        sd = S.op("s", lambda E, st=st: E.activation(st[:, 6:7], st[:, 5:6], AF.Sqrt), inc="SCA")
        S.wait("v", "SCA", sd)
        S.op("v", lambda E, st=st: E.reciprocal(st[:, 7:8], st[:, 6:7]), inc="VEC"); S.hop("v")
        nrm = S.op("v", lambda E, xb=xb, st=st: E.tensor_scalar(
            xnb[:, :], xb, st[:, 3:4], st[:, 7:8], ALU.subtract, ALU.mult), inc="VEC")
        if lnh is not None:
            k8 = ln_state["ctr"] - 1
            S.op("v", lambda E, st=st, k8=k8: E.tensor_copy(lnh[:, k8 * 8:k8 * 8 + 8], st), inc="VEC")
        release(("xst", par), "VEC", nrm)
        for gblk in range(4):
            tb = pAb[:, (gblk % 2) * 1024:(gblk % 2) * 1024 + 1024]
            need("t", ("trbank", gblk % 2))
            S.wait("t", "VEC", nrm)
            pe_last = None
            for j in range(8):
                dm = gblk * 8 + j
                pe_last = S.op("t", lambda E, dm=dm, tb=tb, j=j: E.transpose(
                    tb[:, j * 128:(j + 1) * 128], xnb[:, dm * 128:(dm + 1) * 128], identb[:, :]), inc="PE")
            S.wait("v", "PE", pe_last)
            base0 = xnT_base(gblk * 8, ti)
            ev = S.op("v", lambda E, tb=tb, base0=base0: E.tensor_copy(
                bass.AP(bigT, base0, [[32768, 128], [1024, 8], [1, 128]]), tb[:, 0:1024]), inc="VEC")
            release(("trbank", gblk % 2), "VEC", ev)
        ln_state["last_tr"] = S.c["PE"]

    def stream_weights(dram, col0):
        vals = []
        for wh in range(2):
            need("g", ("wb", wh))
            src = bass.AP(dram, (wh * 2048) * dram.shape[1] + col0,
                          [[dram.shape[1], 128], [128 * dram.shape[1], 16], [1, 256]])
            dst = bass.AP(wb[wh], 0, [[4096, 128], [256, 16], [1, 256]])
            vals.append(dma("g", dst, src, "WB%d" % wh))
        return vals

    def proj_og(par, gd_after, no_stop=False):
        pX = pA if par == 0 else pB
        pe_last = None
        for wh in range(2):
            S.wait("t", "WB%d" % wh, gd_after[wh])
            for dm in range(16):
                dmg = wh * 16 + dm
                for t in range(8):
                    st_ = (wh == 0 and dm == 0 and t % 2 == 0)
                    sp_ = (wh == 1 and dm == 15) and not no_stop
                    pe_last = S.op("t", lambda E, pX=pX, t=t, dmg=dmg, dm=dm, wh=wh, st_=st_, sp_=sp_:
                                   E.matmul(pX[:, t * 256:(t + 1) * 256],
                                            bigT[:, xnT_base(dmg, t):xnT_base(dmg, t) + 128],
                                            wb[wh][:, dm * 256:(dm + 1) * 256],
                                            start=st_, stop=sp_, skip_group_check=True), inc="PE")
            release(("wb", wh), "PE", pe_last)
        return pe_last

    for h in range(2):
        for i in range(8):
            lv = 8 * h + i
            par = i % 2
            need("g", ("xst", par))
            gd_x = dma("g", xst[par], xspan[lv * 128:(lv + 1) * 128, :], "XST%d" % par)
            ln_tile(par, i, "XST%d" % par, gd_x)
        S.barrier()
        if h == 0:
            tap("xnT0", bigT[:, :], [128, 32768], BF16)

        og_list = list(range(24)) if h == 1 else list(range(16, 24))
        pend = None

        def emit_tr_evac(og, par, kind, rope_done):
            S.wait("t", "VEC", rope_done)
            pXb_ = pAb if par == 0 else pBb
            pe_tr = None
            for t in range(8):
                for hh in range(2):
                    j = t * 2 + hh
                    pe_tr = S.op("t", lambda E, j=j, pXb_=pXb_: E.transpose(
                        pXb_[:, j * 128:(j + 1) * 128],
                        qrope[:, j * 128:(j + 1) * 128], identb[:, :]), inc="PE")
            release(("qrope",), "PE", pe_tr)
            S.wait("v", "PE", pe_tr)
            if kind == "q":
                need("v", ("qstage",))
                ev = S.op("v", lambda E, pXb_=pXb_: E.tensor_copy(qstage[:, 0:2048], pXb_[:, 0:2048]), inc="VEC")
                release(("pogpar", par), "VEC", ev)
                S.wait("y", "VEC", ev)
                sy = None
                for hh in range(2):
                    dst = bass.AP(qt_scr, (og * 256 + hh * 128) * 1024, [[1024, 128], [128, 8], [1, 128]])
                    src = bass.AP(arA1.bitcast(BF16), 6144 + hh * 128, [[8192, 128], [256, 8], [1, 128]])
                    sy = dma("y", dst, src, "QS")
                release(("qstage",), "QS", sy)
            else:
                ko = og - 16
                dst = bass.AP(kTb, (ko * 2) * 2048 + (8 * h) * 128,
                              [[16384, 128], [2048, 2], [128, 8], [1, 128]])
                srcb = bass.AP(pXb_, 0, [[4096, 128], [128, 2], [256, 8], [1, 128]])
                ev = S.op("v", lambda E, dst=dst, srcb=srcb: E.tensor_copy(dst, srcb), inc="VEC")
                release(("pogpar", par), "VEC", ev)

        for og in og_list:
            par = og % 2
            gd_after = stream_weights(wqkv, og * 256)
            need("t", ("pogpar", par))
            pe_mm = proj_og(par, gd_after)
            if pend is not None:
                emit_tr_evac(*pend)
                pend = None
            pX = pA if par == 0 else pB
            kind = "q" if og < 16 else ("k" if og < 20 else "v")
            if kind in ("q", "k"):
                S.wait("v", "PE", pe_mm)
                need("v", ("qrope",))
                q1 = bass.AP(pX, 0, [[2048, 128], [256, 8], [128, 2], [1, 64]])
                q2 = bass.AP(pX, 64, [[2048, 128], [256, 8], [128, 2], [1, 64]])
                cosv = bass.AP(arB, (8 * h) * 128, [[4096, 128], [128, 8], [64, 2], [1, 64]])
                sinv = bass.AP(arB, 2048 + (8 * h) * 128, [[4096, 128], [128, 8], [64, 2], [1, 64]])
                t1v = bass.AP(arA0, 2048, [[4096, 128], [128, 8], [64, 2], [1, 64]])
                t2v = bass.AP(arA1, 2048, [[4096, 128], [128, 8], [64, 2], [1, 64]])
                qr1 = bass.AP(arA0.bitcast(BF16), 6144, [[8192, 128], [256, 8], [128, 2], [1, 64]])
                qr2 = bass.AP(arA0.bitcast(BF16), 6144 + 64, [[8192, 128], [256, 8], [128, 2], [1, 64]])
                S.op("v", lambda E, q1=q1, cosv=cosv, t1v=t1v: E.tensor_tensor(t1v, q1, cosv, ALU.mult), inc="VEC")
                S.op("v", lambda E, q2=q2, sinv=sinv, t2v=t2v: E.tensor_tensor(t2v, q2, sinv, ALU.mult), inc="VEC")
                S.op("v", lambda E, t1v=t1v, t2v=t2v, qr1=qr1: E.tensor_tensor(qr1, t1v, t2v, ALU.subtract), inc="VEC")
                S.op("v", lambda E, q1=q1, sinv=sinv, t1v=t1v: E.tensor_tensor(t1v, q1, sinv, ALU.mult), inc="VEC")
                S.op("v", lambda E, q2=q2, cosv=cosv, t2v=t2v: E.tensor_tensor(t2v, q2, cosv, ALU.mult), inc="VEC")
                rope_done = S.op("v", lambda E, t1v=t1v, t2v=t2v, qr2=qr2: E.tensor_tensor(qr2, t1v, t2v, ALU.add), inc="VEC")
                pend = (og, par, kind, rope_done)
            else:
                vo = og - 20
                S.wait("s", "PE", pe_mm)
                dst = bass.AP(Vb, (8 * h) * 1024 + vo * 256, [[16384, 128], [1024, 8], [1, 256]])
                src = bass.AP(pX, 0, [[2048, 128], [256, 8], [1, 256]])
                ev = S.op("s", lambda E, dst=dst, src=src: E.copy(dst, src), inc="SCA")
                release(("pogpar", par), "SCA", ev)
        if pend is not None:
            emit_tr_evac(*pend)
            pend = None
        S.barrier()

    tap("kT", kTb[:, :], [128, 16384], BF16)
    tap("V", Vb[:, :], [128, 16384], BF16)

    ptb4 = [arA0b[:, i * 512:(i + 1) * 512] for i in range(4)]
    psT3 = [pA[:, 0:512], pA[:, 512:1024], pA[:, 1024:1536]]
    psums2 = [pA[0:1, 1536:2048], pB[0:1, 1536:2048]]
    pav2 = [pB[:, 0:512], pB[:, 512:1024]]
    pbc = pB[:, 1024:1536]
    LAG = 2

    jobs = []
    for t in range(NT_OWN):
        for hk in range(NKVH):
            for c in range(9):
                jobs.append((t, hk, c))

    exp_of = {}
    gate_of = {}
    qt_of = {}
    pd_of = {}

    def att_loads(t):
        qsem = "QT%d" % (t % 2)
        need("y", ("qtt", t % 2))
        src = bass.AP(qt_scr, t * 128, [[1024, 128], [128 * 1024, 32], [1, 128]])
        dst = bass.AP(arB.bitcast(BF16), (t % 2) * 4096, [[8192, 128], [128, 32], [1, 128]])
        qt_of[t] = dma("y", dst, src, qsem)
        need("y", ("padt", t % 2))
        pd_of[t] = dma("y", xnbf[0:1, (t % 2) * 512:(t % 2) * 512 + 512], padc_d[t:t + 1, :], qsem)

    att_loads(0)
    att_loads(1)

    def emit_mm(idx):
        t, hk, c = jobs[idx]
        bank = idx % 3
        buf = idx % 4
        pair = t * NKVH + hk
        par = pair % 2
        lvk = t + c
        if idx >= 3:
            S.wait("t", "SCA", exp_of[idx - 3])
        if c == 0 and hk == 0:
            S.wait("t", "QT%d" % (t % 2), pd_of[t])
        qt_rhs = qtt[t % 2][:, (4 * hk) * 128:(4 * hk) * 128 + 512]
        mm = S.op("t", lambda E, bank=bank, hk=hk, lvk=lvk, qt_rhs=qt_rhs:
                  E.matmul(psT3[bank], kTb[:, hk * 2048 + lvk * 128:hk * 2048 + lvk * 128 + 128],
                           qt_rhs, start=True, stop=True, skip_group_check=True), inc="PE")
        S.wait("s", "PE", mm)
        need("s", ("ptb", buf))
        ex = S.op("s", lambda E, buf=buf, bank=bank: E.activation(
            ptb4[buf], psT3[bank], AF.Exp, scale=SM_SCALE), inc="SCA")
        exp_of[idx] = ex
        gate_of[idx] = ("SCA", ex)
        if c == 0 or c == 8:
            S.wait("v", "SCA", ex)
            mk = masksb if c == 0 else maskcb
            mv = S.op("v", lambda E, buf=buf, mk=mk: E.tensor_tensor(ptb4[buf], ptb4[buf], mk[:, :], ALU.mult), inc="VEC")
            gate_of[idx] = ("VEC", mv)

    def emit_acc(idx):
        t, hk, c = jobs[idx]
        buf = idx % 4
        pair = t * NKVH + hk
        par = pair % 2
        lvk = t + c
        gs, gv = gate_of[idx]
        S.wait("t", gs, gv)
        if c == 0:
            need("t", ("psums", par))
            need("t", ("pav", par))
        S.op("t", lambda E, c=c, par=par: E.matmul(
            psums2[par], onescol[:, :], ptb4[buf], start=(c == 0), stop=(c == 8),
            skip_group_check=True), inc="PE")
        av = S.op("t", lambda E, c=c, par=par, hk=hk, lvk=lvk, buf=buf: E.matmul(
            pav2[par], Vb[:, lvk * 1024 + hk * 128:lvk * 1024 + hk * 128 + 128],
            ptb4[buf], start=(c == 0), stop=(c == 8), skip_group_check=True), inc="PE")
        release(("ptb", buf), "PE", av)
        if c == 8:
            finalize_pair(t, hk)

    def finalize_pair(t, hk):
        pair = t * NKVH + hk
        par = pair % 2
        qsem = "QT%d" % (t % 2)
        pe_pair_last = S.c["PE"]
        S.wait("v", "PE", pe_pair_last)
        S.wait("v", qsem, pd_of[t])
        S.op("v", lambda E, par=par, t=t: E.tensor_tensor(
            rsb, psums2[par], xnbf[0:1, (t % 2) * 512:(t % 2) * 512 + 512], ALU.subtract), inc="VEC"); S.hop("v")
        rr = S.op("v", lambda E: E.reciprocal(rsb2, rsb), inc="VEC"); S.hop("v")
        release(("psums", par), "VEC", rr)
        if hk == NKVH - 1:
            release(("padt", t % 2), "VEC", rr)
        S.wait("t", "VEC", rr)
        need("t", ("pbc",))
        bc = S.op("t", lambda E: E.matmul(pbc, onesrow[0:1, :], rsb2, start=True, stop=True,
                                          skip_group_check=True), inc="PE")
        S.wait("s", "PE", bc)
        need("s", ("recipb", par))
        cp = S.op("s", lambda E, par=par: E.copy(recipb[par], pbc), inc="SCA")
        release(("pbc",), "SCA", cp)
        S.wait("v", "SCA", cp)
        dstb = bass.AP(bigT, (4 * hk * 8 + t) * 128, [[32768, 128], [1024, 4], [1, 128]])
        ev = S.op("v", lambda E, par=par, dstb=dstb: E.tensor_tensor(
            dstb, pav2[par], recipb[par], ALU.mult), inc="VEC")
        release(("pav", par), "VEC", ev)
        release(("recipb", par), "VEC", ev)
        if hk == NKVH - 1:
            release(("qtt", t % 2), "PE", pe_pair_last)
            if t + 2 < NT_OWN:
                att_loads(t + 2)

    for idx in range(len(jobs)):
        emit_mm(idx)
        if idx >= LAG:
            emit_acc(idx - LAG)
    for idx in range(len(jobs) - LAG, len(jobs)):
        emit_acc(idx)
    S.barrier()
    tap("attnT", bigT[:, :], [128, 32768], BF16)

    for og in range(16):
        par = og % 2
        gd_after = stream_weights(wo_d, og * 256)
        need("y", ("colst", par))
        csrc = bass.AP(xspan, (8 * 128) * D + og * 256, [[D, 128], [128 * D, 8], [1, 256]])
        cdst = bass.AP(arB, par * 2048, [[4096, 128], [256, 8], [1, 256]])
        sy_col = dma("y", cdst, csrc, "CL%d" % par)
        need("t", ("pogpar", par))
        pe_mm = proj_og(par, gd_after)
        pX = pA if par == 0 else pB
        S.wait("v", "PE", pe_mm)
        S.wait("v", "CL%d" % par, sy_col)
        need("v", ("ostage", par))
        ev = S.op("v", lambda E, pX=pX, par=par: E.tensor_tensor(ostage[par], pX[:, 0:2048], colst[par], ALU.add), inc="VEC")
        release(("pogpar", par), "VEC", ev)
        release(("colst", par), "VEC", ev)
        S.wait("y", "VEC", ev)
        dst = bass.AP(x2_scr, og * 256, [[D, 128], [128 * D, 8], [1, 256]])
        src = bass.AP(arA0 if par == 0 else arA1, 0, [[4096, 128], [256, 8], [1, 256]])
        syx2 = dma("y", dst, src, "OS%d" % par)
        release(("ostage", par), "OS%d" % par, syx2)
    S.barrier()

    adst = bass.AP(kTb, acat_off, [[16384, 128], [48, 32], [1, 48]])
    asrc = bass.AP(acat_d, 0, [[48, 128], [128 * 48, 32], [1, 48]])
    gd_ac = dma("g", adst, asrc, "CST")
    gd_bc = dma("g", kTb[0:49, 0:4096], bcat_d[:, :], "CST")
    gd_ac = gd_bc = dma("g", kTb[48:49, zTs_off:zTs_off + 1024], ones_d[:, :], "CST")

    for t in range(NT_OWN):
        par = t % 2
        need("g", ("xst", par))
        gd_x = dma("g", xst[par], x2_scr[t * 128:(t + 1) * 128, :], "XST%d" % par)
        ln_tile(par, t, "XST%d" % par, gd_x)
        S.wait("t", "CST", gd_ac)
        need("t", ("pz",))
        pz = pB[:, 1536:1584]
        pe_z = None
        for dm in range(32):
            pe_z = S.op("t", lambda E, dm=dm, t=t, pz=pz: E.matmul(
                pz, bigT[:, xnT_base(dm, t):xnT_base(dm, t) + 128],
                kTb[:, acat_off + dm * 48:acat_off + dm * 48 + 48],
                start=(dm == 0), stop=(dm == 31), skip_group_check=True), inc="PE")
        S.wait("s", "PE", pe_z)
        zb = S.op("s", lambda E, pz=pz: E.copy(kTb[:, zbuf_off:zbuf_off + 48], pz), inc="SCA")
        S.wait("t", "SCA", zb)
        pzT = pBb[:, 3200:3328]
        tz = S.op("t", lambda E, pzT=pzT: E.transpose(pzT[0:48, :], kTb[:, zbuf_off:zbuf_off + 48], identb[:, :]), inc="PE")
        S.wait("v", "PE", tz)
        S.wait("v", "VEC", vw49)
        vz = S.op("v", lambda E, t=t, pzT=pzT: E.tensor_scalar(
            kTb[0:48, zTs_off + t * 128:zTs_off + (t + 1) * 128], pzT[0:48, :], w49[0:48, 0:1], None, ALU.mult), inc="VEC")
        release(("pz",), "VEC", vz)
    ln2_vec = S.c["VEC"]
    S.barrier()
    tap("xn2T", bigT[:, :], [128, 32768], BF16)
    tap("zTs", kTb[0:49, zTs_off:zTs_off + 1024], [49, 1024], BF16)
    tap("w49", w49[:, :], [49, 1], F32)
    tap("lsig", lsif[:, :], [1, 8], F32)
    tap("rscg", rsc[:, :], [1, 8], F32)
    tap("eqvg", eqv[:, :], [49, 1], F32)

    for og in range(16):
        par = og % 2
        gd_after = stream_weights(wft, og * 256)
        need("y", ("colst", par))
        csrc = bass.AP(x2_scr, og * 256, [[D, 128], [128 * D, 8], [1, 256]])
        cdst = bass.AP(arB, par * 2048, [[4096, 128], [256, 8], [1, 256]])
        sy_col = dma("y", cdst, csrc, "CL%d" % par)
        need("t", ("pogpar", par))
        pe_mm = proj_og(par, gd_after, no_stop=True)
        pX = pA if par == 0 else pB
        pe_l = None
        for t in range(8):
            pe_l = S.op("t", lambda E, t=t, og=og, pX=pX: E.matmul(
                pX[:, t * 256:(t + 1) * 256], kTb[0:49, zTs_off + t * 128:zTs_off + (t + 1) * 128],
                kTb[0:49, og * 256:(og + 1) * 256],
                start=False, stop=True, skip_group_check=True), inc="PE")
        S.wait("v", "PE", pe_l)
        S.wait("v", "CL%d" % par, sy_col)
        need("v", ("ostage", par))
        ev = S.op("v", lambda E, pX=pX, par=par: E.tensor_tensor(ostage[par], pX[:, 0:2048], colst[par], ALU.add), inc="VEC")
        release(("pogpar", par), "VEC", ev)
        release(("colst", par), "VEC", ev)
        S.wait("y", "VEC", ev)
        dst = bass.AP(out_d, og * 256, [[D, 128], [128 * D, 8], [1, 256]])
        src = bass.AP(arA0 if par == 0 else arA1, 0, [[4096, 128], [256, 8], [1, 256]])
        syo = dma("y", dst, src, "OS%d" % par)
        release(("ostage", par), "OS%d" % par, syo)

    if lnh is not None:
        tap("lnh", lnh[:, :], [128, 224], F32)
    S.wait("y", "OS0", S.c.get("OS0", 0))
    S.wait("y", "OS1", S.c.get("OS1", 0))

    with ExitStack() as emit_es:
        sems = {n: emit_es.enter_context(nc.semaphore(n)) for n in sorted(S.c)}
        with nc.Block() as block:
            @block.gpsimd
            def _(g):
                S.replay("g", g, sems)

            @block.tensor
            def _(t):
                S.replay("t", t, sems)

            @block.vector
            def _(v):
                S.replay("v", v, sems)

            @block.scalar
            def _(s):
                S.replay("s", s, sems)

            @block.sync
            def _(y):
                S.replay("y", y, sems)
    es.close()
    return nc


_CACHE = {}


def _host_constants():
    if "consts" in _CACHE:
        return _CACHE["consts"]
    ident = np.eye(128, dtype=np.float32)
    cc = np.arange(128)[:, None]
    rr = np.arange(128)[None, :]
    causal = (cc <= rr).astype(np.float32)
    strict = (cc > rr).astype(np.float32)
    maskc = np.tile(causal, (1, 4)).astype(np.float32)
    masks = np.tile(strict, (1, 4)).astype(np.float32)
    a2v = np.zeros((49, 2), dtype=np.float32)
    a2v[0:48, 0] = (np.arange(48) // 16 + 2).astype(np.float32)
    a2v[48, 0] = -1.0
    a2v[48, 1] = 1.0
    _CACHE["consts"] = (ident, maskc, masks, a2v)
    return _CACHE["consts"]


def prep_in_maps(L0_attn_x, L0_attn_wq, L0_attn_wk, L0_attn_wv, L0_attn_wo,
                 L0_ffn_weight, L0_ffn_bias, L0_ffn_lora_A, L0_ffn_lora_B,
                 L0_ffn_lora_start_indices):
    x = np.ascontiguousarray(np.asarray(L0_attn_x, dtype=np.float32))
    wqkv = np.ascontiguousarray(np.concatenate(
        [np.asarray(L0_attn_wq, np.float32), np.asarray(L0_attn_wk, np.float32),
         np.asarray(L0_attn_wv, np.float32)], axis=1))
    wo = np.ascontiguousarray(np.asarray(L0_attn_wo, np.float32))
    wft = np.ascontiguousarray(np.asarray(L0_ffn_weight, np.float32).T)
    acat = np.ascontiguousarray(np.asarray(L0_ffn_lora_A, np.float32).transpose(2, 0, 1).reshape(D, 48))
    bcatx = np.ascontiguousarray(np.concatenate(
        [np.asarray(L0_ffn_lora_B, np.float32).transpose(0, 2, 1).reshape(48, D),
         np.asarray(L0_ffn_bias, np.float32)[None, :]], axis=0))
    lsi = np.ascontiguousarray(np.asarray(L0_ffn_lora_start_indices).astype(np.int32).reshape(1, 4))
    ident, maskc, masks, a2v = _host_constants()

    inv_freq = THETA ** (-np.arange(64, dtype=np.float64) / 64.0)

    in_maps = []
    for c in range(8):
        b = c // 4
        Q = c % 4
        if Q == 0:
            xs = np.concatenate([np.zeros((1024, D), np.float32), x[b, 0:1024]], axis=0)
        else:
            xs = np.ascontiguousarray(x[b, (Q - 1) * 1024:(Q + 1) * 1024])
        span0 = Q * 1024 - 1024
        pos = np.maximum(span0 + np.arange(2048), 0).astype(np.float64)
        ang = pos[:, None] * inv_freq[None, :]
        cosv = np.cos(ang).astype(np.float32).reshape(16, 128, 64)
        sinv = np.sin(ang).astype(np.float32).reshape(16, 128, 64)
        cost = np.zeros((128, 16, 2, 64), np.float32)
        sint = np.zeros((128, 16, 2, 64), np.float32)
        for hh in range(2):
            cost[:, :, hh, :] = cosv.transpose(1, 0, 2)
            sint[:, :, hh, :] = sinv.transpose(1, 0, 2)
        cost = np.ascontiguousarray(cost.reshape(128, 2048))
        sint = np.ascontiguousarray(sint.reshape(128, 2048))
        padc = np.zeros((8, 512), np.float32)
        if Q == 0:
            rq = np.arange(128, dtype=np.float32)
            for t in range(8):
                padc[t] = np.tile((127.0 - rq) + 128.0 * (7 - t), 4)
        bq = np.full((1, 1), float(b), np.float32)
        onesd = np.ones((1, 1024), np.float32)
        in_maps.append({
            "xspan": xs, "wqkv": wqkv, "wo": wo, "wft": wft, "acat": acat,
            "bcatx": bcatx, "lsi": lsi, "cost": cost, "sint": sint,
            "padc": padc, "ident": ident, "maskc": maskc, "masks": masks,
            "a2v": a2v, "bq": bq, "onesd": onesd,
        })
    return in_maps


def kernel(**inputs):
    in_maps = prep_in_maps(**inputs)
    if "nc" not in _CACHE:
        _CACHE["nc"] = build()
    nc = _CACHE["nc"]
    res = run_bass_kernel_spmd(nc, in_maps, core_ids=list(range(8)))
    out = np.empty((B, SEQ, D), np.float32)
    for c in range(8):
        b = c // 4
        Q = c % 4
        out[b, Q * 1024:(Q + 1) * 1024] = res.results[c]["out"]
    return out


# version 06
# speedup vs baseline: 1.5216x; 1.0281x over previous
import numpy as np
import ml_dtypes
from contextlib import ExitStack

import concourse.bass as bass
import concourse.mybir as mybir
from concourse.bass_utils import run_bass_kernel_spmd

F32 = mybir.dt.float32
BF16 = mybir.dt.bfloat16
I32 = mybir.dt.int32
AF = mybir.ActivationFunctionType
ALU = mybir.AluOpType
AX = mybir.AxisListType

D = 4096
HD = 128
NKVH = 8
THETA = 10000.0
SM_SCALE = HD ** -0.5
EPS = 1e-5
LORA_SCALE = 1.0
NT_OWN = 8
NT_SPAN = 16
SEQ = 4096
B = 2


class Sched:
    ENGS = ("g", "t", "v", "s", "y")

    def __init__(self):
        self.q = {e: [] for e in self.ENGS}
        self.c = {"PE": 0, "VEC": 0, "SCA": 0}
        self.waited = {}

    def op(self, eng, fn, inc=None, amt=1):
        if inc is not None:
            self.c[inc] = self.c.get(inc, 0) + amt
        self.q[eng].append((fn, inc, amt))
        return self.c[inc] if inc else None

    def wait(self, eng, sem, val):
        if val <= 0:
            return
        key = (eng, sem)
        if self.waited.get(key, 0) >= val:
            return
        self.waited[key] = val
        self.q[eng].append(("W", sem, val))

    def hop(self, eng):
        sem = {"t": "PE", "v": "VEC", "s": "SCA"}[eng]
        self.wait(eng, sem, self.c[sem])

    def barrier(self):
        snap = dict(self.c)
        for e in self.ENGS:
            for sm in snap:
                self.wait(e, sm, snap[sm])

    def replay(self, eng, handle, sems):
        for item in self.q[eng]:
            if item[0] == "W":
                handle.wait_ge(sems[item[1]], item[2])
            else:
                fn, inc, amt = item
                ins = fn(handle)
                if inc is not None:
                    ins.then_inc(sems[inc], amt)


def build(taps=()):
    nc = bass.Bass("TRN2", target_bir_lowering=False)

    xspan = nc.dram_tensor("xspan", [NT_SPAN * 128, D], F32, kind="ExternalInput")
    wqkv = nc.dram_tensor("wqkv", [D, 6144], F32, kind="ExternalInput")
    wo_d = nc.dram_tensor("wo", [D, D], F32, kind="ExternalInput")
    wft = nc.dram_tensor("wft", [D, D], F32, kind="ExternalInput")
    acat_d = nc.dram_tensor("acat", [D, 48], F32, kind="ExternalInput")
    bcat_d = nc.dram_tensor("bcatx", [49, D], F32, kind="ExternalInput")
    lsi_d = nc.dram_tensor("lsi", [1, 4], I32, kind="ExternalInput")
    cost_d = nc.dram_tensor("cost", [128, 2048], F32, kind="ExternalInput")
    sint_d = nc.dram_tensor("sint", [128, 2048], F32, kind="ExternalInput")
    padc_d = nc.dram_tensor("padc", [NT_OWN, 512], F32, kind="ExternalInput")
    ident_d = nc.dram_tensor("ident", [128, 128], F32, kind="ExternalInput")
    maskc_d = nc.dram_tensor("maskc", [128, 512], F32, kind="ExternalInput")
    masks_d = nc.dram_tensor("masks", [128, 512], F32, kind="ExternalInput")
    a2v_d = nc.dram_tensor("a2v", [49, 2], F32, kind="ExternalInput")
    bq_d = nc.dram_tensor("bq", [1, 1], F32, kind="ExternalInput")
    ones_d = nc.dram_tensor("onesd", [1, 1024], F32, kind="ExternalInput")
    out_d = nc.dram_tensor("out", [NT_OWN * 128, D], F32, kind="ExternalOutput")

    qt_scr = nc.dram_tensor("qt_scr", [D, NT_OWN * 128], BF16,
                            kind="ExternalOutput" if "qt" in taps else "Internal")
    x2_scr = nc.dram_tensor("x2_scr", [NT_OWN * 128, D], F32,
                            kind="ExternalOutput" if "x2" in taps else "Internal")

    es = ExitStack()
    sb = lambda n, s, d: es.enter_context(nc.sbuf_tensor(n, s, d))
    ps = lambda n, s, d=F32: es.enter_context(nc.psum_tensor(n, s, d))

    bigT = sb("bigT", [128, 32768], BF16)
    kTb = sb("kTb", [128, 16384], BF16)
    Vb = sb("Vb", [128, 16384], BF16)
    arA0 = sb("arA0", [128, 4096], F32)
    arA1 = sb("arA1", [128, 4096], F32)
    arB = sb("arB", [128, 4096], F32)
    wb = [sb("wb0", [128, 4096], BF16), sb("wb1", [128, 4096], BF16)]
    xnb = sb("xnb", [128, D], BF16)
    identb = sb("identb", [128, 128], BF16)
    maskcb = sb("maskcb", [128, 512], BF16)
    masksb = sb("masksb", [128, 512], BF16)
    onescol = sb("onescol", [128, 1], BF16)
    onesrow = sb("onesrow", [1, 128], F32)
    stats = sb("stats", [128, 16], F32)
    lsib = sb("lsib", [1, 4], I32)
    lsif = sb("lsif", [1, 8], F32)
    rsc = sb("rsc", [1, 8], F32)
    a2vb = sb("a2vb", [49, 2], F32)
    w49 = sb("w49", [49, 1], F32)
    eqv = sb("eqv", [49, 1], F32)
    bqb = sb("bqb", [1, 1], F32)
    lnh = sb("lnh", [128, 224], F32) if "lnh" in taps else None

    rsb = arA1[0:1, 1024:1536]
    rsb2 = arA1[0:1, 1536:2048]
    xnbf = xnb.bitcast(F32)
    padtile = xnbf[0:1, 0:1024]
    arA0b = arA0.bitcast(BF16)
    arA1b = arA1.bitcast(BF16)
    arBb = arB.bitcast(BF16)
    xst = [arA0b[:, 0:4096], arA1b[:, 0:4096]]
    qrope = arA0b[:, 6144:8192]
    qstage = arA1b[:, 6144:8192]
    costb = arB[:, 0:2048]
    sintb = arB[:, 2048:4096]
    ptb = [arA0b[:, 0:512], arA0b[:, 512:1024], arA0b[:, 1024:1536]]
    recipb = [arA1[:, 0:512], arA1[:, 512:1024]]
    qtt = [arBb[:, 0:4096], arBb[:, 4096:8192]]
    colst = [arB[:, 0:2048], arB[:, 2048:4096]]
    ostage = [arA0[:, 0:2048], arA1[:, 0:2048]]
    acat_off = 4096
    zTs_off = 6144
    zbuf_off = 7168

    pA = ps("pA", [128, 2048])
    pB = ps("pB", [128, 2048])
    pAb = pA.bitcast(BF16)
    pBb = pB.bitcast(BF16)

    S = Sched()
    free = {}

    def need(eng, tok):
        if tok in free:
            sem, val = free[tok]
            S.wait(eng, sem, val)

    def release(tok, sem, val):
        free[tok] = (sem, val)

    def dma(eng_q, out_ap, in_ap, sem):
        return S.op(eng_q, lambda E, o=out_ap, i=in_ap: E.dma_start(o, i), inc=sem, amt=16)

    def tap(name, region, shape, dt):
        if name not in taps:
            return
        od = nc.dram_tensor("tap_" + name, shape, dt, kind="ExternalOutput")
        S.barrier()
        dma("y", od[:, :], region, "TP")
        S.barrier()

    dma("g", identb[:, :], ident_d[:, :], "CST")
    dma("g", maskcb[:, :], maskc_d[:, :], "CST")
    dma("g", masksb[:, :], masks_d[:, :], "CST")
    dma("g", costb, cost_d[:, :], "CST")
    dma("g", sintb, sint_d[:, :], "CST")
    dma("g", lsib[:, :], lsi_d[:, :], "CST")
    dma("g", a2vb[:, :], a2v_d[:, :], "CST")
    gd_setup = dma("g", bqb[:, :], bq_d[:, :], "CST")

    S.op("v", lambda E: E.memset(onescol[:, :], 1.0), inc="VEC")
    S.op("v", lambda E: E.memset(onesrow[:, :], 1.0), inc="VEC")
    S.wait("v", "CST", gd_setup)
    S.op("v", lambda E: E.tensor_copy(lsif[:, 0:4], lsib[:, :]), inc="VEC"); S.hop("v")
    S.op("v", lambda E: E.tensor_scalar(lsif[:, 4:8], lsif[:, 0:4], bqb[0:1, 0:1], None, ALU.is_gt), inc="VEC"); S.hop("v")
    c0, c1, c2, c3 = (lsif[:, 4 + i:5 + i] for i in range(4))
    S.op("v", lambda E: E.tensor_tensor(rsc[:, 0:1], c0, c1, ALU.mult), inc="VEC"); S.hop("v")
    S.op("v", lambda E: E.tensor_tensor(rsc[:, 1:2], rsc[:, 0:1], c1, ALU.add), inc="VEC"); S.hop("v")
    S.op("v", lambda E: E.tensor_scalar(rsc[:, 2:3], rsc[:, 1:2], -1.0, 2.0, ALU.mult, ALU.add), inc="VEC"); S.hop("v")
    S.op("v", lambda E: E.tensor_scalar(rsc[:, 3:4], c3, -1.0, 4.0, ALU.mult, ALU.add), inc="VEC"); S.hop("v")
    S.op("v", lambda E: E.tensor_tensor(rsc[:, 4:5], rsc[:, 2:3], rsc[:, 3:4], ALU.subtract), inc="VEC"); S.hop("v")
    S.op("v", lambda E: E.tensor_tensor(rsc[:, 5:6], c2, rsc[:, 4:5], ALU.mult), inc="VEC"); S.hop("v")
    vr = S.op("v", lambda E: E.tensor_tensor(rsc[:, 6:7], rsc[:, 5:6], rsc[:, 3:4], ALU.add), inc="VEC"); S.hop("v")
    S.wait("t", "VEC", vr)
    p_r = S.op("t", lambda E: E.matmul(pB[0:49, 0:1], onesrow[0:1, 0:49], rsc[:, 6:7],
                                       start=True, stop=True, skip_group_check=True), inc="PE")
    S.wait("v", "PE", p_r)
    S.op("v", lambda E: E.tensor_tensor(eqv[:, :], pB[0:49, 0:1], a2vb[:, 0:1], ALU.is_equal), inc="VEC"); S.hop("v")
    S.op("v", lambda E: E.tensor_scalar(eqv[:, :], eqv[:, :], LORA_SCALE, None, ALU.mult), inc="VEC"); S.hop("v")
    vw49 = S.op("v", lambda E: E.tensor_tensor(w49[:, :], eqv[:, :], a2vb[:, 1:2], ALU.add), inc="VEC"); S.hop("v")

    xnT_base = lambda dm, t: (dm * 8 + t) * 128
    ln_state = {"ctr": 0, "last_tr": 0}

    def ln_tile(par, ti, xsem, xval):
        st = stats[:, (ln_state["ctr"] % 2) * 8:(ln_state["ctr"] % 2) * 8 + 8]
        ln_state["ctr"] += 1
        xb = xst[par]
        S.wait("v", xsem, xval)
        S.wait("s", xsem, xval)
        S.op("v", lambda E, xb=xb, st=st: E.reduce_sum(st[:, 0:1], xb, AX.X), inc="VEC")
        S.wait("s", "PE", ln_state["last_tr"])
        sq = S.op("s", lambda E, xb=xb, st=st: E.activation(
            xnb[:, :], xb, AF.Square, accum_out=st[:, 1:2]), inc="SCA"); S.hop("s")
        S.op("s", lambda E: E.nop(), inc="SCA")
        S.wait("v", "SCA", S.c["SCA"])
        S.op("v", lambda E, st=st: E.tensor_scalar(st[:, 2:3], st[:, 1:2], 1.0 / D, EPS, ALU.mult, ALU.add), inc="VEC")
        S.op("v", lambda E, st=st: E.tensor_scalar(st[:, 3:4], st[:, 0:1], 1.0 / D, None, ALU.mult), inc="VEC"); S.hop("v")
        S.op("v", lambda E, st=st: E.tensor_tensor(st[:, 4:5], st[:, 3:4], st[:, 3:4], ALU.mult), inc="VEC"); S.hop("v")
        vv = S.op("v", lambda E, st=st: E.tensor_tensor(st[:, 5:6], st[:, 2:3], st[:, 4:5], ALU.subtract), inc="VEC"); S.hop("v")
        S.wait("s", "VEC", vv)
        sd = S.op("s", lambda E, st=st: E.activation(st[:, 6:7], st[:, 5:6], AF.Sqrt), inc="SCA")
        S.wait("v", "SCA", sd)
        S.op("v", lambda E, st=st: E.reciprocal(st[:, 7:8], st[:, 6:7]), inc="VEC"); S.hop("v")
        nrm = S.op("v", lambda E, xb=xb, st=st: E.tensor_scalar(
            xnb[:, :], xb, st[:, 3:4], st[:, 7:8], ALU.subtract, ALU.mult), inc="VEC")
        if lnh is not None:
            k8 = ln_state["ctr"] - 1
            S.op("v", lambda E, st=st, k8=k8: E.tensor_copy(lnh[:, k8 * 8:k8 * 8 + 8], st), inc="VEC")
        release(("xst", par), "VEC", nrm)
        for gblk in range(4):
            tb = pAb[:, (gblk % 2) * 1024:(gblk % 2) * 1024 + 1024]
            need("t", ("trbank", gblk % 2))
            S.wait("t", "VEC", nrm)
            pe_last = None
            for j in range(8):
                dm = gblk * 8 + j
                pe_last = S.op("t", lambda E, dm=dm, tb=tb, j=j: E.transpose(
                    tb[:, j * 128:(j + 1) * 128], xnb[:, dm * 128:(dm + 1) * 128], identb[:, :]), inc="PE")
            S.wait("s", "PE", pe_last)
            base0 = xnT_base(gblk * 8, ti)
            ev = S.op("s", lambda E, tb=tb, base0=base0: E.copy(
                bass.AP(bigT, base0, [[32768, 128], [1024, 8], [1, 128]]), tb[:, 0:1024]), inc="SCA")
            release(("trbank", gblk % 2), "SCA", ev)
        ln_state["last_tr"] = S.c["PE"]

    def stream_weights(dram, col0):
        vals = []
        for wh in range(2):
            need("g", ("wb", wh))
            src = bass.AP(dram, (wh * 2048) * dram.shape[1] + col0,
                          [[dram.shape[1], 128], [128 * dram.shape[1], 16], [1, 256]])
            dst = bass.AP(wb[wh], 0, [[4096, 128], [256, 16], [1, 256]])
            vals.append(dma("g", dst, src, "WB%d" % wh))
        return vals

    def proj_og(par, gd_after, no_stop=False):
        pX = pA if par == 0 else pB
        pe_last = None
        for wh in range(2):
            S.wait("t", "WB%d" % wh, gd_after[wh])
            for dm in range(16):
                dmg = wh * 16 + dm
                for t in range(8):
                    st_ = (wh == 0 and dm == 0 and t % 2 == 0)
                    sp_ = (wh == 1 and dm == 15) and not no_stop
                    pe_last = S.op("t", lambda E, pX=pX, t=t, dmg=dmg, dm=dm, wh=wh, st_=st_, sp_=sp_:
                                   E.matmul(pX[:, t * 256:(t + 1) * 256],
                                            bigT[:, xnT_base(dmg, t):xnT_base(dmg, t) + 128],
                                            wb[wh][:, dm * 256:(dm + 1) * 256],
                                            start=st_, stop=sp_, skip_group_check=True), inc="PE")
            release(("wb", wh), "PE", pe_last)
        return pe_last

    for h in range(2):
        for i in range(8):
            lv = 8 * h + i
            par = i % 2
            need("g", ("xst", par))
            gd_x = dma("g", xst[par], xspan[lv * 128:(lv + 1) * 128, :], "XST%d" % par)
            ln_tile(par, i, "XST%d" % par, gd_x)
        S.barrier()
        if h == 0:
            tap("xnT0", bigT[:, :], [128, 32768], BF16)

        og_list = list(range(24)) if h == 1 else list(range(16, 24))
        pend = None

        def emit_tr_evac(og, par, kind, rope_done):
            S.wait("t", "VEC", rope_done)
            pXb_ = pAb if par == 0 else pBb
            pe_tr = None
            for t in range(8):
                for hh in range(2):
                    j = t * 2 + hh
                    pe_tr = S.op("t", lambda E, j=j, pXb_=pXb_: E.transpose(
                        pXb_[:, j * 128:(j + 1) * 128],
                        qrope[:, j * 128:(j + 1) * 128], identb[:, :]), inc="PE")
            release(("qrope",), "PE", pe_tr)
            S.wait("v", "PE", pe_tr)
            if kind == "q":
                need("v", ("qstage",))
                ev = S.op("v", lambda E, pXb_=pXb_: E.tensor_copy(qstage[:, 0:2048], pXb_[:, 0:2048]), inc="VEC")
                release(("pogpar", par), "VEC", ev)
                S.wait("y", "VEC", ev)
                sy = None
                for hh in range(2):
                    dst = bass.AP(qt_scr, (og * 256 + hh * 128) * 1024, [[1024, 128], [128, 8], [1, 128]])
                    src = bass.AP(arA1.bitcast(BF16), 6144 + hh * 128, [[8192, 128], [256, 8], [1, 128]])
                    sy = dma("y", dst, src, "QS")
                release(("qstage",), "QS", sy)
            else:
                ko = og - 16
                dst = bass.AP(kTb, (ko * 2) * 2048 + (8 * h) * 128,
                              [[16384, 128], [2048, 2], [128, 8], [1, 128]])
                srcb = bass.AP(pXb_, 0, [[4096, 128], [128, 2], [256, 8], [1, 128]])
                ev = S.op("v", lambda E, dst=dst, srcb=srcb: E.tensor_copy(dst, srcb), inc="VEC")
                release(("pogpar", par), "VEC", ev)

        for og in og_list:
            par = og % 2
            gd_after = stream_weights(wqkv, og * 256)
            need("t", ("pogpar", par))
            pe_mm = proj_og(par, gd_after)
            if pend is not None:
                emit_tr_evac(*pend)
                pend = None
            pX = pA if par == 0 else pB
            kind = "q" if og < 16 else ("k" if og < 20 else "v")
            if kind in ("q", "k"):
                S.wait("v", "PE", pe_mm)
                need("v", ("qrope",))
                q1 = bass.AP(pX, 0, [[2048, 128], [256, 8], [128, 2], [1, 64]])
                q2 = bass.AP(pX, 64, [[2048, 128], [256, 8], [128, 2], [1, 64]])
                cosv = bass.AP(arB, (8 * h) * 128, [[4096, 128], [128, 8], [64, 2], [1, 64]])
                sinv = bass.AP(arB, 2048 + (8 * h) * 128, [[4096, 128], [128, 8], [64, 2], [1, 64]])
                t1v = bass.AP(arA0, 2048, [[4096, 128], [128, 8], [64, 2], [1, 64]])
                t2v = bass.AP(arA1, 2048, [[4096, 128], [128, 8], [64, 2], [1, 64]])
                qr1 = bass.AP(arA0.bitcast(BF16), 6144, [[8192, 128], [256, 8], [128, 2], [1, 64]])
                qr2 = bass.AP(arA0.bitcast(BF16), 6144 + 64, [[8192, 128], [256, 8], [128, 2], [1, 64]])
                S.op("v", lambda E, q1=q1, cosv=cosv, t1v=t1v: E.tensor_tensor(t1v, q1, cosv, ALU.mult), inc="VEC")
                S.op("v", lambda E, q2=q2, sinv=sinv, t2v=t2v: E.tensor_tensor(t2v, q2, sinv, ALU.mult), inc="VEC")
                S.op("v", lambda E, t1v=t1v, t2v=t2v, qr1=qr1: E.tensor_tensor(qr1, t1v, t2v, ALU.subtract), inc="VEC")
                S.op("v", lambda E, q1=q1, sinv=sinv, t1v=t1v: E.tensor_tensor(t1v, q1, sinv, ALU.mult), inc="VEC")
                S.op("v", lambda E, q2=q2, cosv=cosv, t2v=t2v: E.tensor_tensor(t2v, q2, cosv, ALU.mult), inc="VEC")
                rope_done = S.op("v", lambda E, t1v=t1v, t2v=t2v, qr2=qr2: E.tensor_tensor(qr2, t1v, t2v, ALU.add), inc="VEC")
                pend = (og, par, kind, rope_done)
            else:
                vo = og - 20
                S.wait("s", "PE", pe_mm)
                dst = bass.AP(Vb, (8 * h) * 1024 + vo * 256, [[16384, 128], [1024, 8], [1, 256]])
                src = bass.AP(pX, 0, [[2048, 128], [256, 8], [1, 256]])
                ev = S.op("s", lambda E, dst=dst, src=src: E.copy(dst, src), inc="SCA")
                release(("pogpar", par), "SCA", ev)
        if pend is not None:
            emit_tr_evac(*pend)
            pend = None
        S.barrier()

    tap("kT", kTb[:, :], [128, 16384], BF16)
    tap("V", Vb[:, :], [128, 16384], BF16)

    ptb4 = [arA0b[:, i * 512:(i + 1) * 512] for i in range(4)]
    psT3 = [pA[:, 0:512], pA[:, 512:1024], pA[:, 1024:1536]]
    psums2 = [pA[0:1, 1536:2048], pB[0:1, 1536:2048]]
    pav2 = [pB[:, 0:512], pB[:, 512:1024]]
    pbc = pB[:, 1024:1536]
    LAG = 3

    jobs = []
    for t in range(NT_OWN):
        for hk in range(NKVH):
            for c in range(9):
                jobs.append((t, hk, c))

    exp_of = {}
    gate_of = {}
    qt_of = {}
    pd_of = {}

    def att_loads(t):
        qsem = "QT%d" % (t % 2)
        need("y", ("qtt", t % 2))
        src = bass.AP(qt_scr, t * 128, [[1024, 128], [128 * 1024, 32], [1, 128]])
        dst = bass.AP(arB.bitcast(BF16), (t % 2) * 4096, [[8192, 128], [128, 32], [1, 128]])
        qt_of[t] = dma("y", dst, src, qsem)
        need("y", ("padt", t % 2))
        pd_of[t] = dma("y", xnbf[0:1, (t % 2) * 512:(t % 2) * 512 + 512], padc_d[t:t + 1, :], qsem)

    att_loads(0)
    att_loads(1)

    def emit_mm(idx):
        t, hk, c = jobs[idx]
        bank = idx % 3
        buf = idx % 4
        pair = t * NKVH + hk
        par = pair % 2
        lvk = t + c
        if idx >= 3:
            S.wait("t", "SCA", exp_of[idx - 3])
        if c == 0 and hk == 0:
            S.wait("t", "QT%d" % (t % 2), pd_of[t])
        qt_rhs = qtt[t % 2][:, (4 * hk) * 128:(4 * hk) * 128 + 512]
        mm = S.op("t", lambda E, bank=bank, hk=hk, lvk=lvk, qt_rhs=qt_rhs:
                  E.matmul(psT3[bank], kTb[:, hk * 2048 + lvk * 128:hk * 2048 + lvk * 128 + 128],
                           qt_rhs, start=True, stop=True, skip_group_check=True), inc="PE")
        S.wait("s", "PE", mm)
        need("s", ("ptb", buf))
        ex = S.op("s", lambda E, buf=buf, bank=bank: E.activation(
            ptb4[buf], psT3[bank], AF.Exp, scale=SM_SCALE), inc="SCA")
        exp_of[idx] = ex
        gate_of[idx] = ("SCA", ex)
        if c == 0 or c == 8:
            S.wait("v", "SCA", ex)
            mk = masksb if c == 0 else maskcb
            mv = S.op("v", lambda E, buf=buf, mk=mk: E.tensor_tensor(ptb4[buf], ptb4[buf], mk[:, :], ALU.mult), inc="VEC")
            gate_of[idx] = ("VEC", mv)

    def emit_acc(idx):
        t, hk, c = jobs[idx]
        buf = idx % 4
        pair = t * NKVH + hk
        par = pair % 2
        lvk = t + c
        gs, gv = gate_of[idx]
        S.wait("t", gs, gv)
        if c == 0:
            need("t", ("psums", par))
            need("t", ("pav", par))
        S.op("t", lambda E, c=c, par=par: E.matmul(
            psums2[par], onescol[:, :], ptb4[buf], start=(c == 0), stop=(c == 8),
            skip_group_check=True), inc="PE")
        av = S.op("t", lambda E, c=c, par=par, hk=hk, lvk=lvk, buf=buf: E.matmul(
            pav2[par], Vb[:, lvk * 1024 + hk * 128:lvk * 1024 + hk * 128 + 128],
            ptb4[buf], start=(c == 0), stop=(c == 8), skip_group_check=True), inc="PE")
        release(("ptb", buf), "PE", av)
        if c == 8:
            finalize_pair(t, hk)

    def finalize_pair(t, hk):
        pair = t * NKVH + hk
        par = pair % 2
        qsem = "QT%d" % (t % 2)
        pe_pair_last = S.c["PE"]
        S.wait("v", "PE", pe_pair_last)
        S.wait("v", qsem, pd_of[t])
        S.op("v", lambda E, par=par, t=t: E.tensor_tensor(
            rsb, psums2[par], xnbf[0:1, (t % 2) * 512:(t % 2) * 512 + 512], ALU.subtract), inc="VEC"); S.hop("v")
        rr = S.op("v", lambda E: E.reciprocal(rsb2, rsb), inc="VEC"); S.hop("v")
        release(("psums", par), "VEC", rr)
        if hk == NKVH - 1:
            release(("padt", t % 2), "VEC", rr)
        S.wait("t", "VEC", rr)
        need("t", ("pbc",))
        bc = S.op("t", lambda E: E.matmul(pbc, onesrow[0:1, :], rsb2, start=True, stop=True,
                                          skip_group_check=True), inc="PE")
        S.wait("s", "PE", bc)
        need("s", ("recipb", par))
        cp = S.op("s", lambda E, par=par: E.copy(recipb[par], pbc), inc="SCA")
        release(("pbc",), "SCA", cp)
        S.wait("v", "SCA", cp)
        dstb = bass.AP(bigT, (4 * hk * 8 + t) * 128, [[32768, 128], [1024, 4], [1, 128]])
        ev = S.op("v", lambda E, par=par, dstb=dstb: E.tensor_tensor(
            dstb, pav2[par], recipb[par], ALU.mult), inc="VEC")
        release(("pav", par), "VEC", ev)
        release(("recipb", par), "VEC", ev)
        if hk == NKVH - 1:
            release(("qtt", t % 2), "PE", pe_pair_last)
            if t + 2 < NT_OWN:
                att_loads(t + 2)

    for idx in range(len(jobs)):
        emit_mm(idx)
        if idx >= LAG:
            emit_acc(idx - LAG)
    for idx in range(len(jobs) - LAG, len(jobs)):
        emit_acc(idx)
    S.barrier()
    tap("attnT", bigT[:, :], [128, 32768], BF16)

    for og in range(16):
        par = og % 2
        gd_after = stream_weights(wo_d, og * 256)
        need("y", ("colst", par))
        csrc = bass.AP(xspan, (8 * 128) * D + og * 256, [[D, 128], [128 * D, 8], [1, 256]])
        cdst = bass.AP(arB, par * 2048, [[4096, 128], [256, 8], [1, 256]])
        sy_col = dma("y", cdst, csrc, "CL%d" % par)
        need("t", ("pogpar", par))
        pe_mm = proj_og(par, gd_after)
        pX = pA if par == 0 else pB
        S.wait("v", "PE", pe_mm)
        S.wait("v", "CL%d" % par, sy_col)
        need("v", ("ostage", par))
        ev = S.op("v", lambda E, pX=pX, par=par: E.tensor_tensor(ostage[par], pX[:, 0:2048], colst[par], ALU.add), inc="VEC")
        release(("pogpar", par), "VEC", ev)
        release(("colst", par), "VEC", ev)
        S.wait("y", "VEC", ev)
        dst = bass.AP(x2_scr, og * 256, [[D, 128], [128 * D, 8], [1, 256]])
        src = bass.AP(arA0 if par == 0 else arA1, 0, [[4096, 128], [256, 8], [1, 256]])
        syx2 = dma("y", dst, src, "OS%d" % par)
        release(("ostage", par), "OS%d" % par, syx2)
    S.barrier()

    adst = bass.AP(kTb, acat_off, [[16384, 128], [48, 32], [1, 48]])
    asrc = bass.AP(acat_d, 0, [[48, 128], [128 * 48, 32], [1, 48]])
    gd_ac = dma("g", adst, asrc, "CST")
    gd_bc = dma("g", kTb[0:49, 0:4096], bcat_d[:, :], "CST")
    gd_ac = gd_bc = dma("g", kTb[48:49, zTs_off:zTs_off + 1024], ones_d[:, :], "CST")

    for t in range(NT_OWN):
        par = t % 2
        need("g", ("xst", par))
        gd_x = dma("g", xst[par], x2_scr[t * 128:(t + 1) * 128, :], "XST%d" % par)
        ln_tile(par, t, "XST%d" % par, gd_x)
        S.wait("t", "CST", gd_ac)
        need("t", ("pz",))
        pz = pB[:, 1536:1584]
        pe_z = None
        for dm in range(32):
            pe_z = S.op("t", lambda E, dm=dm, t=t, pz=pz: E.matmul(
                pz, bigT[:, xnT_base(dm, t):xnT_base(dm, t) + 128],
                kTb[:, acat_off + dm * 48:acat_off + dm * 48 + 48],
                start=(dm == 0), stop=(dm == 31), skip_group_check=True), inc="PE")
        S.wait("s", "PE", pe_z)
        zb = S.op("s", lambda E, pz=pz: E.copy(kTb[:, zbuf_off:zbuf_off + 48], pz), inc="SCA")
        S.wait("t", "SCA", zb)
        pzT = pBb[:, 3200:3328]
        tz = S.op("t", lambda E, pzT=pzT: E.transpose(pzT[0:48, :], kTb[:, zbuf_off:zbuf_off + 48], identb[:, :]), inc="PE")
        S.wait("v", "PE", tz)
        S.wait("v", "VEC", vw49)
        vz = S.op("v", lambda E, t=t, pzT=pzT: E.tensor_scalar(
            kTb[0:48, zTs_off + t * 128:zTs_off + (t + 1) * 128], pzT[0:48, :], w49[0:48, 0:1], None, ALU.mult), inc="VEC")
        release(("pz",), "VEC", vz)
    ln2_vec = S.c["VEC"]
    S.barrier()
    tap("xn2T", bigT[:, :], [128, 32768], BF16)
    tap("zTs", kTb[0:49, zTs_off:zTs_off + 1024], [49, 1024], BF16)
    tap("w49", w49[:, :], [49, 1], F32)
    tap("lsig", lsif[:, :], [1, 8], F32)
    tap("rscg", rsc[:, :], [1, 8], F32)
    tap("eqvg", eqv[:, :], [49, 1], F32)

    for og in range(16):
        par = og % 2
        gd_after = stream_weights(wft, og * 256)
        need("y", ("colst", par))
        csrc = bass.AP(x2_scr, og * 256, [[D, 128], [128 * D, 8], [1, 256]])
        cdst = bass.AP(arB, par * 2048, [[4096, 128], [256, 8], [1, 256]])
        sy_col = dma("y", cdst, csrc, "CL%d" % par)
        need("t", ("pogpar", par))
        pe_mm = proj_og(par, gd_after, no_stop=True)
        pX = pA if par == 0 else pB
        pe_l = None
        for t in range(8):
            pe_l = S.op("t", lambda E, t=t, og=og, pX=pX: E.matmul(
                pX[:, t * 256:(t + 1) * 256], kTb[0:49, zTs_off + t * 128:zTs_off + (t + 1) * 128],
                kTb[0:49, og * 256:(og + 1) * 256],
                start=False, stop=True, skip_group_check=True), inc="PE")
        S.wait("v", "PE", pe_l)
        S.wait("v", "CL%d" % par, sy_col)
        need("v", ("ostage", par))
        ev = S.op("v", lambda E, pX=pX, par=par: E.tensor_tensor(ostage[par], pX[:, 0:2048], colst[par], ALU.add), inc="VEC")
        release(("pogpar", par), "VEC", ev)
        release(("colst", par), "VEC", ev)
        S.wait("y", "VEC", ev)
        dst = bass.AP(out_d, og * 256, [[D, 128], [128 * D, 8], [1, 256]])
        src = bass.AP(arA0 if par == 0 else arA1, 0, [[4096, 128], [256, 8], [1, 256]])
        syo = dma("y", dst, src, "OS%d" % par)
        release(("ostage", par), "OS%d" % par, syo)

    if lnh is not None:
        tap("lnh", lnh[:, :], [128, 224], F32)
    S.wait("y", "OS0", S.c.get("OS0", 0))
    S.wait("y", "OS1", S.c.get("OS1", 0))

    with ExitStack() as emit_es:
        sems = {n: emit_es.enter_context(nc.semaphore(n)) for n in sorted(S.c)}
        with nc.Block() as block:
            @block.gpsimd
            def _(g):
                S.replay("g", g, sems)

            @block.tensor
            def _(t):
                S.replay("t", t, sems)

            @block.vector
            def _(v):
                S.replay("v", v, sems)

            @block.scalar
            def _(s):
                S.replay("s", s, sems)

            @block.sync
            def _(y):
                S.replay("y", y, sems)
    es.close()
    return nc


_CACHE = {}


def _host_constants():
    if "consts" in _CACHE:
        return _CACHE["consts"]
    ident = np.eye(128, dtype=np.float32)
    cc = np.arange(128)[:, None]
    rr = np.arange(128)[None, :]
    causal = (cc <= rr).astype(np.float32)
    strict = (cc > rr).astype(np.float32)
    maskc = np.tile(causal, (1, 4)).astype(np.float32)
    masks = np.tile(strict, (1, 4)).astype(np.float32)
    a2v = np.zeros((49, 2), dtype=np.float32)
    a2v[0:48, 0] = (np.arange(48) // 16 + 2).astype(np.float32)
    a2v[48, 0] = -1.0
    a2v[48, 1] = 1.0
    _CACHE["consts"] = (ident, maskc, masks, a2v)
    return _CACHE["consts"]


def prep_in_maps(L0_attn_x, L0_attn_wq, L0_attn_wk, L0_attn_wv, L0_attn_wo,
                 L0_ffn_weight, L0_ffn_bias, L0_ffn_lora_A, L0_ffn_lora_B,
                 L0_ffn_lora_start_indices):
    x = np.ascontiguousarray(np.asarray(L0_attn_x, dtype=np.float32))
    wqkv = np.ascontiguousarray(np.concatenate(
        [np.asarray(L0_attn_wq, np.float32), np.asarray(L0_attn_wk, np.float32),
         np.asarray(L0_attn_wv, np.float32)], axis=1))
    wo = np.ascontiguousarray(np.asarray(L0_attn_wo, np.float32))
    wft = np.ascontiguousarray(np.asarray(L0_ffn_weight, np.float32).T)
    acat = np.ascontiguousarray(np.asarray(L0_ffn_lora_A, np.float32).transpose(2, 0, 1).reshape(D, 48))
    bcatx = np.ascontiguousarray(np.concatenate(
        [np.asarray(L0_ffn_lora_B, np.float32).transpose(0, 2, 1).reshape(48, D),
         np.asarray(L0_ffn_bias, np.float32)[None, :]], axis=0))
    lsi = np.ascontiguousarray(np.asarray(L0_ffn_lora_start_indices).astype(np.int32).reshape(1, 4))
    ident, maskc, masks, a2v = _host_constants()

    inv_freq = THETA ** (-np.arange(64, dtype=np.float64) / 64.0)

    in_maps = []
    for c in range(8):
        b = c // 4
        Q = c % 4
        if Q == 0:
            xs = np.concatenate([np.zeros((1024, D), np.float32), x[b, 0:1024]], axis=0)
        else:
            xs = np.ascontiguousarray(x[b, (Q - 1) * 1024:(Q + 1) * 1024])
        span0 = Q * 1024 - 1024
        pos = np.maximum(span0 + np.arange(2048), 0).astype(np.float64)
        ang = pos[:, None] * inv_freq[None, :]
        cosv = np.cos(ang).astype(np.float32).reshape(16, 128, 64)
        sinv = np.sin(ang).astype(np.float32).reshape(16, 128, 64)
        cost = np.zeros((128, 16, 2, 64), np.float32)
        sint = np.zeros((128, 16, 2, 64), np.float32)
        for hh in range(2):
            cost[:, :, hh, :] = cosv.transpose(1, 0, 2)
            sint[:, :, hh, :] = sinv.transpose(1, 0, 2)
        cost = np.ascontiguousarray(cost.reshape(128, 2048))
        sint = np.ascontiguousarray(sint.reshape(128, 2048))
        padc = np.zeros((8, 512), np.float32)
        if Q == 0:
            rq = np.arange(128, dtype=np.float32)
            for t in range(8):
                padc[t] = np.tile((127.0 - rq) + 128.0 * (7 - t), 4)
        bq = np.full((1, 1), float(b), np.float32)
        onesd = np.ones((1, 1024), np.float32)
        in_maps.append({
            "xspan": xs, "wqkv": wqkv, "wo": wo, "wft": wft, "acat": acat,
            "bcatx": bcatx, "lsi": lsi, "cost": cost, "sint": sint,
            "padc": padc, "ident": ident, "maskc": maskc, "masks": masks,
            "a2v": a2v, "bq": bq, "onesd": onesd,
        })
    return in_maps


def kernel(**inputs):
    in_maps = prep_in_maps(**inputs)
    if "nc" not in _CACHE:
        _CACHE["nc"] = build()
    nc = _CACHE["nc"]
    res = run_bass_kernel_spmd(nc, in_maps, core_ids=list(range(8)))
    out = np.empty((B, SEQ, D), np.float32)
    for c in range(8):
        b = c // 4
        Q = c % 4
        out[b, Q * 1024:(Q + 1) * 1024] = res.results[c]["out"]
    return out


# version 07 (final)
# speedup vs baseline: 1.7570x; 1.1547x over previous
import numpy as np
import ml_dtypes
from contextlib import ExitStack

import concourse.bass as bass
import concourse.mybir as mybir
from concourse.bass_utils import run_bass_kernel_spmd

F32 = mybir.dt.float32
BF16 = mybir.dt.bfloat16
I32 = mybir.dt.int32
AF = mybir.ActivationFunctionType
ALU = mybir.AluOpType
AX = mybir.AxisListType

D = 4096
HD = 128
NKVH = 8
THETA = 10000.0
SM_SCALE = HD ** -0.5
EPS = 1e-5
LORA_SCALE = 1.0
NT_OWN = 8
NT_SPAN = 16
SEQ = 4096
B = 2


class Sched:
    ENGS = ("g", "t", "v", "s", "y")

    def __init__(self):
        self.q = {e: [] for e in self.ENGS}
        self.c = {"PE": 0, "VEC": 0, "SCA": 0}
        self.waited = {}

    def op(self, eng, fn, inc=None, amt=1):
        if inc is not None:
            self.c[inc] = self.c.get(inc, 0) + amt
        self.q[eng].append((fn, inc, amt))
        return self.c[inc] if inc else None

    def wait(self, eng, sem, val):
        if val <= 0:
            return
        key = (eng, sem)
        if self.waited.get(key, 0) >= val:
            return
        self.waited[key] = val
        self.q[eng].append(("W", sem, val))

    def hop(self, eng):
        sem = {"t": "PE", "v": "VEC", "s": "SCA"}[eng]
        self.wait(eng, sem, self.c[sem])

    def barrier(self):
        snap = dict(self.c)
        for e in self.ENGS:
            for sm in snap:
                self.wait(e, sm, snap[sm])

    def replay(self, eng, handle, sems):
        for item in self.q[eng]:
            if item[0] == "W":
                handle.wait_ge(sems[item[1]], item[2])
            else:
                fn, inc, amt = item
                ins = fn(handle)
                if inc is not None:
                    ins.then_inc(sems[inc], amt)


def build(taps=()):
    nc = bass.Bass("TRN2", target_bir_lowering=False)

    xspan = nc.dram_tensor("xspan", [NT_SPAN * 128, D], F32, kind="ExternalInput")
    wqkv = nc.dram_tensor("wqkv", [D, 6144], F32, kind="ExternalInput")
    wo_d = nc.dram_tensor("wo", [D, D], F32, kind="ExternalInput")
    wft = nc.dram_tensor("wft", [D, D], F32, kind="ExternalInput")
    acat_d = nc.dram_tensor("acat", [D, 48], F32, kind="ExternalInput")
    bcat_d = nc.dram_tensor("bcatx", [49, D], F32, kind="ExternalInput")
    lsi_d = nc.dram_tensor("lsi", [1, 4], I32, kind="ExternalInput")
    cost_d = nc.dram_tensor("cost", [128, 2048], F32, kind="ExternalInput")
    sint_d = nc.dram_tensor("sint", [128, 2048], F32, kind="ExternalInput")
    padc_d = nc.dram_tensor("padc", [NT_OWN, 512], F32, kind="ExternalInput")
    ident_d = nc.dram_tensor("ident", [128, 128], F32, kind="ExternalInput")
    maskc_d = nc.dram_tensor("maskc", [128, 512], F32, kind="ExternalInput")
    masks_d = nc.dram_tensor("masks", [128, 512], F32, kind="ExternalInput")
    a2v_d = nc.dram_tensor("a2v", [49, 2], F32, kind="ExternalInput")
    bq_d = nc.dram_tensor("bq", [1, 1], F32, kind="ExternalInput")
    ones_d = nc.dram_tensor("onesd", [1, 1024], F32, kind="ExternalInput")
    out_d = nc.dram_tensor("out", [NT_OWN * 128, D], F32, kind="ExternalOutput")

    qt_scr = nc.dram_tensor("qt_scr", [D, NT_OWN * 128], BF16,
                            kind="ExternalOutput" if "qt" in taps else "Internal")
    x2_scr = nc.dram_tensor("x2_scr", [NT_OWN * 128, D], F32,
                            kind="ExternalOutput" if "x2" in taps else "Internal")

    es = ExitStack()
    sb = lambda n, s, d: es.enter_context(nc.sbuf_tensor(n, s, d))
    ps = lambda n, s, d=F32: es.enter_context(nc.psum_tensor(n, s, d))

    bigT = sb("bigT", [128, 32768], BF16)
    kTb = sb("kTb", [128, 16384], BF16)
    Vb = sb("Vb", [128, 16384], BF16)
    arA0 = sb("arA0", [128, 4096], F32)
    arA1 = sb("arA1", [128, 4096], F32)
    arB = sb("arB", [128, 4096], F32)
    wb = [sb("wb0", [128, 4096], BF16), sb("wb1", [128, 4096], BF16)]
    xnb = sb("xnb", [128, D], BF16)
    identb = sb("identb", [128, 128], BF16)
    maskcb = sb("maskcb", [128, 512], BF16)
    masksb = sb("masksb", [128, 512], BF16)
    onescol = sb("onescol", [128, 1], BF16)
    onesrow = sb("onesrow", [1, 128], F32)
    onesrowb = sb("onesrowb", [1, 128], BF16)
    stats = sb("stats", [128, 16], F32)
    lsib = sb("lsib", [1, 4], I32)
    lsif = sb("lsif", [1, 8], F32)
    rsc = sb("rsc", [1, 8], F32)
    a2vb = sb("a2vb", [49, 2], F32)
    w49 = sb("w49", [49, 1], F32)
    eqv = sb("eqv", [49, 1], F32)
    bqb = sb("bqb", [1, 1], F32)
    lnh = sb("lnh", [128, 224], F32) if "lnh" in taps else None

    rsb = arA1[0:1, 1024:1536]
    rsb2 = arA1[0:1, 1536:2048]
    xnbf = xnb.bitcast(F32)
    padtile = xnbf[0:1, 0:1024]
    arA0b = arA0.bitcast(BF16)
    arA1b = arA1.bitcast(BF16)
    arBb = arB.bitcast(BF16)
    xst = [arA0b[:, 0:4096], arA1b[:, 0:4096]]
    qrope = arA0b[:, 6144:8192]
    qstage = arA1b[:, 6144:8192]
    costb = arB[:, 0:2048]
    sintb = arB[:, 2048:4096]
    ptb = [arA0b[:, 0:512], arA0b[:, 512:1024], arA0b[:, 1024:1536]]
    recipb = [arA1[:, 0:512], arA1[:, 512:1024]]
    qtt = [arBb[:, 0:4096], arBb[:, 4096:8192]]
    colst = [arB[:, 0:2048], arB[:, 2048:4096]]
    ostage = [arA0[:, 0:2048], arA1[:, 0:2048]]
    acat_off = 4096
    zTs_off = 6144
    zbuf_off = 7168

    pA = ps("pA", [128, 2048])
    pB = ps("pB", [128, 2048])
    pAb = pA.bitcast(BF16)
    pBb = pB.bitcast(BF16)

    S = Sched()
    free = {}

    def need(eng, tok):
        if tok in free:
            sem, val = free[tok]
            S.wait(eng, sem, val)

    def release(tok, sem, val):
        free[tok] = (sem, val)

    def dma(eng_q, out_ap, in_ap, sem):
        return S.op(eng_q, lambda E, o=out_ap, i=in_ap: E.dma_start(o, i), inc=sem, amt=16)

    def tap(name, region, shape, dt):
        if name not in taps:
            return
        od = nc.dram_tensor("tap_" + name, shape, dt, kind="ExternalOutput")
        S.barrier()
        dma("y", od[:, :], region, "TP")
        S.barrier()

    dma("g", identb[:, :], ident_d[:, :], "CST")
    dma("g", maskcb[:, :], maskc_d[:, :], "CST")
    dma("g", masksb[:, :], masks_d[:, :], "CST")
    dma("g", costb, cost_d[:, :], "CST")
    dma("g", sintb, sint_d[:, :], "CST")
    dma("g", lsib[:, :], lsi_d[:, :], "CST")
    dma("g", a2vb[:, :], a2v_d[:, :], "CST")
    gd_setup = dma("g", bqb[:, :], bq_d[:, :], "CST")

    S.op("v", lambda E: E.memset(onescol[:, :], 1.0), inc="VEC")
    S.op("v", lambda E: E.memset(onesrow[:, :], 1.0), inc="VEC")
    S.op("v", lambda E: E.memset(onesrowb[:, :], 1.0), inc="VEC")
    S.wait("v", "CST", gd_setup)
    S.op("v", lambda E: E.tensor_copy(lsif[:, 0:4], lsib[:, :]), inc="VEC"); S.hop("v")
    S.op("v", lambda E: E.tensor_scalar(lsif[:, 4:8], lsif[:, 0:4], bqb[0:1, 0:1], None, ALU.is_gt), inc="VEC"); S.hop("v")
    c0, c1, c2, c3 = (lsif[:, 4 + i:5 + i] for i in range(4))
    S.op("v", lambda E: E.tensor_tensor(rsc[:, 0:1], c0, c1, ALU.mult), inc="VEC"); S.hop("v")
    S.op("v", lambda E: E.tensor_tensor(rsc[:, 1:2], rsc[:, 0:1], c1, ALU.add), inc="VEC"); S.hop("v")
    S.op("v", lambda E: E.tensor_scalar(rsc[:, 2:3], rsc[:, 1:2], -1.0, 2.0, ALU.mult, ALU.add), inc="VEC"); S.hop("v")
    S.op("v", lambda E: E.tensor_scalar(rsc[:, 3:4], c3, -1.0, 4.0, ALU.mult, ALU.add), inc="VEC"); S.hop("v")
    S.op("v", lambda E: E.tensor_tensor(rsc[:, 4:5], rsc[:, 2:3], rsc[:, 3:4], ALU.subtract), inc="VEC"); S.hop("v")
    S.op("v", lambda E: E.tensor_tensor(rsc[:, 5:6], c2, rsc[:, 4:5], ALU.mult), inc="VEC"); S.hop("v")
    vr = S.op("v", lambda E: E.tensor_tensor(rsc[:, 6:7], rsc[:, 5:6], rsc[:, 3:4], ALU.add), inc="VEC"); S.hop("v")
    S.wait("t", "VEC", vr)
    p_r = S.op("t", lambda E: E.matmul(pB[0:49, 0:1], onesrow[0:1, 0:49], rsc[:, 6:7],
                                       start=True, stop=True, skip_group_check=True), inc="PE")
    S.wait("v", "PE", p_r)
    S.op("v", lambda E: E.tensor_tensor(eqv[:, :], pB[0:49, 0:1], a2vb[:, 0:1], ALU.is_equal), inc="VEC"); S.hop("v")
    S.op("v", lambda E: E.tensor_scalar(eqv[:, :], eqv[:, :], LORA_SCALE, None, ALU.mult), inc="VEC"); S.hop("v")
    vw49 = S.op("v", lambda E: E.tensor_tensor(w49[:, :], eqv[:, :], a2vb[:, 1:2], ALU.add), inc="VEC"); S.hop("v")

    xnT_base = lambda dm, t: (dm * 8 + t) * 128
    ln_state = {"ctr": 0, "last_tr": 0}

    def ln_tile(par, ti, xsem, xval):
        st = stats[:, (ln_state["ctr"] % 2) * 8:(ln_state["ctr"] % 2) * 8 + 8]
        ln_state["ctr"] += 1
        xb = xst[par]
        S.wait("v", xsem, xval)
        S.wait("s", xsem, xval)
        S.op("v", lambda E, xb=xb, st=st: E.reduce_sum(st[:, 0:1], xb, AX.X), inc="VEC")
        S.wait("s", "PE", ln_state["last_tr"])
        sq = S.op("s", lambda E, xb=xb, st=st: E.activation(
            xnb[:, :], xb, AF.Square, accum_out=st[:, 1:2]), inc="SCA"); S.hop("s")
        S.op("s", lambda E: E.nop(), inc="SCA")
        S.wait("v", "SCA", S.c["SCA"])
        S.op("v", lambda E, st=st: E.tensor_scalar(st[:, 2:3], st[:, 1:2], 1.0 / D, EPS, ALU.mult, ALU.add), inc="VEC")
        S.op("v", lambda E, st=st: E.tensor_scalar(st[:, 3:4], st[:, 0:1], 1.0 / D, None, ALU.mult), inc="VEC"); S.hop("v")
        S.op("v", lambda E, st=st: E.tensor_tensor(st[:, 4:5], st[:, 3:4], st[:, 3:4], ALU.mult), inc="VEC"); S.hop("v")
        vv = S.op("v", lambda E, st=st: E.tensor_tensor(st[:, 5:6], st[:, 2:3], st[:, 4:5], ALU.subtract), inc="VEC"); S.hop("v")
        S.wait("s", "VEC", vv)
        sd = S.op("s", lambda E, st=st: E.activation(st[:, 6:7], st[:, 5:6], AF.Sqrt), inc="SCA")
        S.wait("v", "SCA", sd)
        S.op("v", lambda E, st=st: E.reciprocal(st[:, 7:8], st[:, 6:7]), inc="VEC"); S.hop("v")
        nrm = S.op("v", lambda E, xb=xb, st=st: E.tensor_scalar(
            xnb[:, :], xb, st[:, 3:4], st[:, 7:8], ALU.subtract, ALU.mult), inc="VEC")
        if lnh is not None:
            k8 = ln_state["ctr"] - 1
            S.op("v", lambda E, st=st, k8=k8: E.tensor_copy(lnh[:, k8 * 8:k8 * 8 + 8], st), inc="VEC")
        release(("xst", par), "VEC", nrm)
        for gblk in range(4):
            tb = pAb[:, (gblk % 2) * 1024:(gblk % 2) * 1024 + 1024]
            need("t", ("trbank", gblk % 2))
            S.wait("t", "VEC", nrm)
            pe_last = None
            for j in range(8):
                dm = gblk * 8 + j
                pe_last = S.op("t", lambda E, dm=dm, tb=tb, j=j: E.transpose(
                    tb[:, j * 128:(j + 1) * 128], xnb[:, dm * 128:(dm + 1) * 128], identb[:, :]), inc="PE")
            S.wait("s", "PE", pe_last)
            base0 = xnT_base(gblk * 8, ti)
            ev = S.op("s", lambda E, tb=tb, base0=base0: E.copy(
                bass.AP(bigT, base0, [[32768, 128], [1024, 8], [1, 128]]), tb[:, 0:1024]), inc="SCA")
            release(("trbank", gblk % 2), "SCA", ev)
        ln_state["last_tr"] = S.c["PE"]

    WPAIR_A = ((wb[0], 0), (wb[1], 0), "WB", "wb")
    WPAIR_WO = ((kTb, 0), (kTb, 8192), "WC", "wbC")
    WPAIR_FF = ((Vb, 0), (Vb, 4096), "WC", "wbC")

    def stream_weights(dram, col0, pair=WPAIR_A):
        b0, b1, semp, tok = pair
        vals = []
        for wh in range(2):
            buf, boff = (b0, b1)[wh]
            need("g", (tok, wh))
            src = bass.AP(dram, (wh * 2048) * dram.shape[1] + col0,
                          [[dram.shape[1], 128], [128 * dram.shape[1], 16], [1, 256]])
            dst = bass.AP(buf, boff, [[buf.shape[1], 128], [256, 16], [1, 256]])
            vals.append(dma("g", dst, src, "%s%d" % (semp, wh)))
        return vals

    def proj_og(par, gd_after, no_stop=False, pair=WPAIR_A):
        b0, b1, semp, tok = pair
        pX = pA if par == 0 else pB
        pe_last = None
        for wh in range(2):
            buf, boff = (b0, b1)[wh]
            S.wait("t", "%s%d" % (semp, wh), gd_after[wh])
            for dm in range(16):
                dmg = wh * 16 + dm
                for t in range(8):
                    st_ = (wh == 0 and dm == 0 and t % 2 == 0)
                    sp_ = (wh == 1 and dm == 15) and not no_stop
                    pe_last = S.op("t", lambda E, pX=pX, t=t, dmg=dmg, dm=dm, st_=st_, sp_=sp_, buf=buf, boff=boff:
                                   E.matmul(pX[:, t * 256:(t + 1) * 256],
                                            bigT[:, xnT_base(dmg, t):xnT_base(dmg, t) + 128],
                                            buf[:, boff + dm * 256:boff + (dm + 1) * 256],
                                            start=st_, stop=sp_, skip_group_check=True), inc="PE")
            release((tok, wh), "PE", pe_last)
        return pe_last

    for h in range(2):
        for i in range(8):
            lv = 8 * h + i
            par = i % 2
            need("g", ("xst", par))
            gd_x = dma("g", xst[par], xspan[lv * 128:(lv + 1) * 128, :], "XST%d" % par)
            ln_tile(par, i, "XST%d" % par, gd_x)
        S.barrier()
        if h == 0:
            tap("xnT0", bigT[:, :], [128, 32768], BF16)

        og_list = list(range(24)) if h == 1 else list(range(16, 24))
        pend = None

        def emit_tr_evac(og, par, kind, rope_done):
            S.wait("t", "VEC", rope_done)
            pXb_ = pAb if par == 0 else pBb
            pe_tr = None
            for t in range(8):
                for hh in range(2):
                    j = t * 2 + hh
                    pe_tr = S.op("t", lambda E, j=j, pXb_=pXb_: E.transpose(
                        pXb_[:, j * 128:(j + 1) * 128],
                        qrope[:, j * 128:(j + 1) * 128], identb[:, :]), inc="PE")
            release(("qrope",), "PE", pe_tr)
            S.wait("v", "PE", pe_tr)
            if kind == "q":
                need("v", ("qstage",))
                ev = S.op("v", lambda E, pXb_=pXb_: E.tensor_copy(qstage[:, 0:2048], pXb_[:, 0:2048]), inc="VEC")
                release(("pogpar", par), "VEC", ev)
                S.wait("y", "VEC", ev)
                sy = None
                for hh in range(2):
                    dst = bass.AP(qt_scr, (og * 256 + hh * 128) * 1024, [[1024, 128], [128, 8], [1, 128]])
                    src = bass.AP(arA1.bitcast(BF16), 6144 + hh * 128, [[8192, 128], [256, 8], [1, 128]])
                    sy = dma("y", dst, src, "QS")
                release(("qstage",), "QS", sy)
            else:
                ko = og - 16
                dst = bass.AP(kTb, (ko * 2) * 2048 + (8 * h) * 128,
                              [[16384, 128], [2048, 2], [128, 8], [1, 128]])
                srcb = bass.AP(pXb_, 0, [[4096, 128], [128, 2], [256, 8], [1, 128]])
                ev = S.op("v", lambda E, dst=dst, srcb=srcb: E.tensor_copy(dst, srcb), inc="VEC")
                release(("pogpar", par), "VEC", ev)

        for og in og_list:
            par = og % 2
            gd_after = stream_weights(wqkv, og * 256)
            need("t", ("pogpar", par))
            pe_mm = proj_og(par, gd_after)
            if pend is not None:
                emit_tr_evac(*pend)
                pend = None
            pX = pA if par == 0 else pB
            kind = "q" if og < 16 else ("k" if og < 20 else "v")
            if kind in ("q", "k"):
                S.wait("v", "PE", pe_mm)
                need("v", ("qrope",))
                q1 = bass.AP(pX, 0, [[2048, 128], [256, 8], [128, 2], [1, 64]])
                q2 = bass.AP(pX, 64, [[2048, 128], [256, 8], [128, 2], [1, 64]])
                cosv = bass.AP(arB, (8 * h) * 128, [[4096, 128], [128, 8], [64, 2], [1, 64]])
                sinv = bass.AP(arB, 2048 + (8 * h) * 128, [[4096, 128], [128, 8], [64, 2], [1, 64]])
                t1v = bass.AP(arA0, 2048, [[4096, 128], [128, 8], [64, 2], [1, 64]])
                t2v = bass.AP(arA1, 2048, [[4096, 128], [128, 8], [64, 2], [1, 64]])
                qr1 = bass.AP(arA0.bitcast(BF16), 6144, [[8192, 128], [256, 8], [128, 2], [1, 64]])
                qr2 = bass.AP(arA0.bitcast(BF16), 6144 + 64, [[8192, 128], [256, 8], [128, 2], [1, 64]])
                S.op("v", lambda E, q1=q1, cosv=cosv, t1v=t1v: E.tensor_tensor(t1v, q1, cosv, ALU.mult), inc="VEC")
                S.op("v", lambda E, q2=q2, sinv=sinv, t2v=t2v: E.tensor_tensor(t2v, q2, sinv, ALU.mult), inc="VEC")
                S.op("v", lambda E, t1v=t1v, t2v=t2v, qr1=qr1: E.tensor_tensor(qr1, t1v, t2v, ALU.subtract), inc="VEC")
                S.op("v", lambda E, q1=q1, sinv=sinv, t1v=t1v: E.tensor_tensor(t1v, q1, sinv, ALU.mult), inc="VEC")
                S.op("v", lambda E, q2=q2, cosv=cosv, t2v=t2v: E.tensor_tensor(t2v, q2, cosv, ALU.mult), inc="VEC")
                rope_done = S.op("v", lambda E, t1v=t1v, t2v=t2v, qr2=qr2: E.tensor_tensor(qr2, t1v, t2v, ALU.add), inc="VEC")
                pend = (og, par, kind, rope_done)
            else:
                vo = og - 20
                S.wait("s", "PE", pe_mm)
                dst = bass.AP(Vb, (8 * h) * 1024 + vo * 256, [[16384, 128], [1024, 8], [1, 256]])
                src = bass.AP(pX, 0, [[2048, 128], [256, 8], [1, 256]])
                ev = S.op("s", lambda E, dst=dst, src=src: E.copy(dst, src), inc="SCA")
                release(("pogpar", par), "SCA", ev)
        if pend is not None:
            emit_tr_evac(*pend)
            pend = None
        S.barrier()

    tap("kT", kTb[:, :], [128, 16384], BF16)
    tap("V", Vb[:, :], [128, 16384], BF16)

    ptb4 = [arA0b[:, i * 512:(i + 1) * 512] for i in range(4)]
    psT3 = [pA[:, 0:512], pA[:, 512:1024], pA[:, 1024:1536]]
    psums = pA[0:1, 1536:2048]
    pav3 = [pB[:, 0:512], pB[:, 512:1024], pB[:, 1024:1536]]
    pbc = pB[:, 1536:2048]
    rsbP = [arA1[0:1, 1024:1536], arA1[0:1, 1536:2048]]
    rsbFP = [arA1[0:1, 2048:2560], arA1[0:1, 2560:3072]]
    rsb2P = [arA1b[0:1, 6144:6656], arA1b[0:1, 6656:7168]]
    LAG = 3

    jobs = []
    for t in range(NT_OWN):
        for hk in range(NKVH):
            for c in range(9):
                jobs.append((t, hk, c))

    exp_of = {}
    gate_of = {}
    qt_of = {}
    pd_of = {}
    sub_of = {}
    recip_of = {}
    bc_of = {}
    cp_of = {}
    mult_of = {}
    pair_pe_last = {}

    def att_loads(t):
        qsem = "QT%d" % (t % 2)
        if t == 0:
            S.wait("y", "QS", S.c.get("QS", 0))
        need("y", ("qtt", t % 2))
        src = bass.AP(qt_scr, t * 128, [[1024, 128], [128 * 1024, 32], [1, 128]])
        dst = bass.AP(arB.bitcast(BF16), (t % 2) * 4096, [[8192, 128], [128, 32], [1, 128]])
        qt_of[t] = dma("y", dst, src, qsem)
        need("y", ("padt", t % 2))
        pd_of[t] = dma("y", xnbf[0:1, (t % 2) * 512:(t % 2) * 512 + 512], padc_d[t:t + 1, :], qsem)

    att_loads(0)
    att_loads(1)
    gd_wo = {0: stream_weights(wo_d, 0)}

    def emit_mm(idx):
        t, hk, c = jobs[idx]
        bank = idx % 3
        buf = idx % 4
        lvk = t + c
        if idx >= 3:
            S.wait("t", "SCA", exp_of[idx - 3])
        if c == 0 and hk == 0:
            S.wait("t", "QT%d" % (t % 2), pd_of[t])
        qt_rhs = qtt[t % 2][:, (4 * hk) * 128:(4 * hk) * 128 + 512]
        mm = S.op("t", lambda E, bank=bank, hk=hk, lvk=lvk, qt_rhs=qt_rhs:
                  E.matmul(psT3[bank], kTb[:, hk * 2048 + lvk * 128:hk * 2048 + lvk * 128 + 128],
                           qt_rhs, start=True, stop=True, skip_group_check=True), inc="PE")
        S.wait("s", "PE", mm)
        need("s", ("ptb", buf))
        ex = S.op("s", lambda E, buf=buf, bank=bank: E.activation(
            ptb4[buf], psT3[bank], AF.Exp, scale=SM_SCALE), inc="SCA")
        exp_of[idx] = ex
        gate_of[idx] = ("SCA", ex)
        if c == 0 or c == 8:
            S.wait("v", "SCA", ex)
            mk = masksb if c == 0 else maskcb
            mv = S.op("v", lambda E, buf=buf, mk=mk: E.tensor_tensor(ptb4[buf], ptb4[buf], mk[:, :], ALU.mult), inc="VEC")
            gate_of[idx] = ("VEC", mv)

    def emit_acc(idx):
        t, hk, c = jobs[idx]
        buf = idx % 4
        p = t * NKVH + hk
        lvk = t + c
        gs, gv = gate_of[idx]
        S.wait("t", gs, gv)
        if c != 0:
            if c == 1:
                if p >= 1:
                    S.wait("t", "VEC", sub_of[p - 1])
                s0 = S.op("t", lambda E, buf4=(idx - 1) % 4: E.matmul(
                    psums, onescol[:, :], ptb4[buf4], start=True, stop=False,
                    skip_group_check=True), inc="PE")
                release(("ptb", (idx - 1) % 4), "PE", s0)
            S.op("t", lambda E, c=c, buf=buf: E.matmul(
                psums, onescol[:, :], ptb4[buf], start=False, stop=(c == 8),
                skip_group_check=True), inc="PE")
        if c == 0 and p >= 3:
            S.wait("t", "VEC", mult_of[p - 3])
        av = S.op("t", lambda E, c=c, p=p, hk=hk, lvk=lvk, buf=buf: E.matmul(
            pav3[p % 3], Vb[:, lvk * 1024 + hk * 128:lvk * 1024 + hk * 128 + 128],
            ptb4[buf], start=(c == 0), stop=(c == 8), skip_group_check=True), inc="PE")
        if c != 0:
            release(("ptb", buf), "PE", av)
        if c == 8:
            pair_pe_last[p] = S.c["PE"]
            emit_f1(p)
            if p >= 1:
                emit_f2(p - 1)

    def emit_f1(p):
        t, hk = p // NKVH, p % NKVH
        qsem = "QT%d" % (t % 2)
        S.wait("v", "PE", pair_pe_last[p])
        S.wait("v", qsem, pd_of[t])
        if p >= 2:
            S.wait("v", "PE", bc_of[p - 2])
        sub_of[p] = S.op("v", lambda E, p=p, t=t: E.tensor_tensor(
            rsbP[p % 2], psums, xnbf[0:1, (t % 2) * 512:(t % 2) * 512 + 512], ALU.subtract), inc="VEC")
        S.hop("v")
        S.op("v", lambda E, p=p: E.reciprocal(rsbFP[p % 2], rsbP[p % 2]), inc="VEC")
        S.hop("v")
        recip_of[p] = S.op("v", lambda E, p=p: E.tensor_copy(rsb2P[p % 2], rsbFP[p % 2]), inc="VEC")
        S.hop("v")
        if hk == NKVH - 1:
            release(("padt", t % 2), "VEC", sub_of[p])
            release(("qtt", t % 2), "PE", pair_pe_last[p])
            if t + 2 < NT_OWN:
                att_loads(t + 2)

    def emit_f2(p):
        t, hk = p // NKVH, p % NKVH
        S.wait("t", "VEC", recip_of[p])
        if p >= 1:
            S.wait("t", "VEC", cp_of[p - 1])
        bc_of[p] = S.op("t", lambda E, p=p: E.matmul(
            pbc, onesrowb[0:1, :], rsb2P[p % 2], start=True, stop=True,
            skip_group_check=True), inc="PE")
        S.wait("v", "PE", bc_of[p])
        cp_of[p] = S.op("v", lambda E, p=p: E.tensor_copy(recipb[p % 2], pbc), inc="VEC")
        dstb = bass.AP(bigT, (4 * hk * 8 + t) * 128, [[32768, 128], [1024, 4], [1, 128]])
        mult_of[p] = S.op("v", lambda E, p=p, dstb=dstb: E.tensor_tensor(
            dstb, pav3[p % 3], recipb[p % 2], ALU.mult), inc="VEC")

    for idx in range(len(jobs)):
        emit_mm(idx)
        if idx >= LAG:
            emit_acc(idx - LAG)
    for idx in range(len(jobs) - LAG, len(jobs)):
        emit_acc(idx)
    emit_f2(NT_OWN * NKVH - 1)
    S.barrier()
    tap("attnT", bigT[:, :], [128, 32768], BF16)

    gd_wo[1] = stream_weights(wo_d, 256, WPAIR_WO)
    gd_ff = {}
    for og in range(16):
        par = og % 2
        wpair = WPAIR_A if par == 0 else WPAIR_WO
        need("y", ("colst", par))
        csrc = bass.AP(xspan, (8 * 128) * D + og * 256, [[D, 128], [128 * D, 8], [1, 256]])
        cdst = bass.AP(arB, par * 2048, [[4096, 128], [256, 8], [1, 256]])
        sy_col = dma("y", cdst, csrc, "CL%d" % par)
        need("t", ("pogpar", par))
        pe_mm = proj_og(par, gd_wo[og], pair=wpair)
        if og + 2 < 16:
            gd_wo[og + 2] = stream_weights(wo_d, (og + 2) * 256, wpair)
        elif og == 14:
            gd_ff[0] = stream_weights(wft, 0)
        elif og == 15:
            gd_ff[1] = stream_weights(wft, 256, WPAIR_FF)
        pX = pA if par == 0 else pB
        S.wait("v", "PE", pe_mm)
        S.wait("v", "CL%d" % par, sy_col)
        need("v", ("ostage", par))
        ev = S.op("v", lambda E, pX=pX, par=par: E.tensor_tensor(ostage[par], pX[:, 0:2048], colst[par], ALU.add), inc="VEC")
        release(("pogpar", par), "VEC", ev)
        release(("colst", par), "VEC", ev)
        S.wait("y", "VEC", ev)
        dst = bass.AP(x2_scr, og * 256, [[D, 128], [128 * D, 8], [1, 256]])
        src = bass.AP(arA0 if par == 0 else arA1, 0, [[4096, 128], [256, 8], [1, 256]])
        syx2 = dma("y", dst, src, "OS%d" % par)
        release(("ostage", par), "OS%d" % par, syx2)
    S.barrier()

    adst = bass.AP(kTb, acat_off, [[16384, 128], [48, 32], [1, 48]])
    asrc = bass.AP(acat_d, 0, [[48, 128], [128 * 48, 32], [1, 48]])
    gd_ac = dma("g", adst, asrc, "CST")
    gd_bc = dma("g", kTb[0:49, 0:4096], bcat_d[:, :], "CST")
    gd_ac = gd_bc = dma("g", kTb[48:49, zTs_off:zTs_off + 1024], ones_d[:, :], "CST")

    S.wait("g", "OS0", S.c.get("OS0", 0))
    S.wait("g", "OS1", S.c.get("OS1", 0))
    for t in range(NT_OWN):
        par = t % 2
        need("g", ("xst", par))
        gd_x = dma("g", xst[par], x2_scr[t * 128:(t + 1) * 128, :], "XST%d" % par)
        ln_tile(par, t, "XST%d" % par, gd_x)
        S.wait("t", "CST", gd_ac)
        need("t", ("pz",))
        pz = pB[:, 1536:1584]
        pe_z = None
        for dm in range(32):
            pe_z = S.op("t", lambda E, dm=dm, t=t, pz=pz: E.matmul(
                pz, bigT[:, xnT_base(dm, t):xnT_base(dm, t) + 128],
                kTb[:, acat_off + dm * 48:acat_off + dm * 48 + 48],
                start=(dm == 0), stop=(dm == 31), skip_group_check=True), inc="PE")
        S.wait("s", "PE", pe_z)
        zb = S.op("s", lambda E, pz=pz: E.copy(kTb[:, zbuf_off:zbuf_off + 48], pz), inc="SCA")
        S.wait("t", "SCA", zb)
        pzT = pBb[:, 3200:3328]
        tz = S.op("t", lambda E, pzT=pzT: E.transpose(pzT[0:48, :], kTb[:, zbuf_off:zbuf_off + 48], identb[:, :]), inc="PE")
        S.wait("v", "PE", tz)
        S.wait("v", "VEC", vw49)
        vz = S.op("v", lambda E, t=t, pzT=pzT: E.tensor_scalar(
            kTb[0:48, zTs_off + t * 128:zTs_off + (t + 1) * 128], pzT[0:48, :], w49[0:48, 0:1], None, ALU.mult), inc="VEC")
        release(("pz",), "VEC", vz)
    ln2_vec = S.c["VEC"]
    S.barrier()
    tap("xn2T", bigT[:, :], [128, 32768], BF16)
    tap("zTs", kTb[0:49, zTs_off:zTs_off + 1024], [49, 1024], BF16)
    tap("w49", w49[:, :], [49, 1], F32)
    tap("lsig", lsif[:, :], [1, 8], F32)
    tap("rscg", rsc[:, :], [1, 8], F32)
    tap("eqvg", eqv[:, :], [49, 1], F32)

    for og in range(16):
        par = og % 2
        wpair = WPAIR_A if par == 0 else WPAIR_FF
        need("y", ("colst", par))
        csrc = bass.AP(x2_scr, og * 256, [[D, 128], [128 * D, 8], [1, 256]])
        cdst = bass.AP(arB, par * 2048, [[4096, 128], [256, 8], [1, 256]])
        sy_col = dma("y", cdst, csrc, "CL%d" % par)
        need("t", ("pogpar", par))
        pe_mm = proj_og(par, gd_ff[og], no_stop=True, pair=wpair)
        if og + 2 < 16:
            gd_ff[og + 2] = stream_weights(wft, (og + 2) * 256, wpair)
        pX = pA if par == 0 else pB
        pe_l = None
        for t in range(8):
            pe_l = S.op("t", lambda E, t=t, og=og, pX=pX: E.matmul(
                pX[:, t * 256:(t + 1) * 256], kTb[0:49, zTs_off + t * 128:zTs_off + (t + 1) * 128],
                kTb[0:49, og * 256:(og + 1) * 256],
                start=False, stop=True, skip_group_check=True), inc="PE")
        S.wait("v", "PE", pe_l)
        S.wait("v", "CL%d" % par, sy_col)
        need("v", ("ostage", par))
        ev = S.op("v", lambda E, pX=pX, par=par: E.tensor_tensor(ostage[par], pX[:, 0:2048], colst[par], ALU.add), inc="VEC")
        release(("pogpar", par), "VEC", ev)
        release(("colst", par), "VEC", ev)
        S.wait("y", "VEC", ev)
        dst = bass.AP(out_d, og * 256, [[D, 128], [128 * D, 8], [1, 256]])
        src = bass.AP(arA0 if par == 0 else arA1, 0, [[4096, 128], [256, 8], [1, 256]])
        syo = dma("y", dst, src, "OS%d" % par)
        release(("ostage", par), "OS%d" % par, syo)

    if lnh is not None:
        tap("lnh", lnh[:, :], [128, 224], F32)
    S.wait("y", "OS0", S.c.get("OS0", 0))
    S.wait("y", "OS1", S.c.get("OS1", 0))

    with ExitStack() as emit_es:
        sems = {n: emit_es.enter_context(nc.semaphore(n)) for n in sorted(S.c)}
        with nc.Block() as block:
            @block.gpsimd
            def _(g):
                S.replay("g", g, sems)

            @block.tensor
            def _(t):
                S.replay("t", t, sems)

            @block.vector
            def _(v):
                S.replay("v", v, sems)

            @block.scalar
            def _(s):
                S.replay("s", s, sems)

            @block.sync
            def _(y):
                S.replay("y", y, sems)
    es.close()
    return nc


_CACHE = {}


def _host_constants():
    if "consts" in _CACHE:
        return _CACHE["consts"]
    ident = np.eye(128, dtype=np.float32)
    cc = np.arange(128)[:, None]
    rr = np.arange(128)[None, :]
    causal = (cc <= rr).astype(np.float32)
    strict = (cc > rr).astype(np.float32)
    maskc = np.tile(causal, (1, 4)).astype(np.float32)
    masks = np.tile(strict, (1, 4)).astype(np.float32)
    a2v = np.zeros((49, 2), dtype=np.float32)
    a2v[0:48, 0] = (np.arange(48) // 16 + 2).astype(np.float32)
    a2v[48, 0] = -1.0
    a2v[48, 1] = 1.0
    _CACHE["consts"] = (ident, maskc, masks, a2v)
    return _CACHE["consts"]


def prep_in_maps(L0_attn_x, L0_attn_wq, L0_attn_wk, L0_attn_wv, L0_attn_wo,
                 L0_ffn_weight, L0_ffn_bias, L0_ffn_lora_A, L0_ffn_lora_B,
                 L0_ffn_lora_start_indices):
    x = np.ascontiguousarray(np.asarray(L0_attn_x, dtype=np.float32))
    wqkv = np.ascontiguousarray(np.concatenate(
        [np.asarray(L0_attn_wq, np.float32), np.asarray(L0_attn_wk, np.float32),
         np.asarray(L0_attn_wv, np.float32)], axis=1))
    wo = np.ascontiguousarray(np.asarray(L0_attn_wo, np.float32))
    wft = np.ascontiguousarray(np.asarray(L0_ffn_weight, np.float32).T)
    acat = np.ascontiguousarray(np.asarray(L0_ffn_lora_A, np.float32).transpose(2, 0, 1).reshape(D, 48))
    bcatx = np.ascontiguousarray(np.concatenate(
        [np.asarray(L0_ffn_lora_B, np.float32).transpose(0, 2, 1).reshape(48, D),
         np.asarray(L0_ffn_bias, np.float32)[None, :]], axis=0))
    lsi = np.ascontiguousarray(np.asarray(L0_ffn_lora_start_indices).astype(np.int32).reshape(1, 4))
    ident, maskc, masks, a2v = _host_constants()

    inv_freq = THETA ** (-np.arange(64, dtype=np.float64) / 64.0)

    in_maps = []
    for c in range(8):
        b = c // 4
        Q = c % 4
        if Q == 0:
            xs = np.concatenate([np.zeros((1024, D), np.float32), x[b, 0:1024]], axis=0)
        else:
            xs = np.ascontiguousarray(x[b, (Q - 1) * 1024:(Q + 1) * 1024])
        span0 = Q * 1024 - 1024
        pos = np.maximum(span0 + np.arange(2048), 0).astype(np.float64)
        ang = pos[:, None] * inv_freq[None, :]
        cosv = np.cos(ang).astype(np.float32).reshape(16, 128, 64)
        sinv = np.sin(ang).astype(np.float32).reshape(16, 128, 64)
        cost = np.zeros((128, 16, 2, 64), np.float32)
        sint = np.zeros((128, 16, 2, 64), np.float32)
        for hh in range(2):
            cost[:, :, hh, :] = cosv.transpose(1, 0, 2)
            sint[:, :, hh, :] = sinv.transpose(1, 0, 2)
        cost = np.ascontiguousarray(cost.reshape(128, 2048))
        sint = np.ascontiguousarray(sint.reshape(128, 2048))
        padc = np.zeros((8, 512), np.float32)
        if Q == 0:
            rq = np.arange(128, dtype=np.float32)
            for t in range(8):
                padc[t] = np.tile((127.0 - rq) + 128.0 * (7 - t), 4)
        bq = np.full((1, 1), float(b), np.float32)
        onesd = np.ones((1, 1024), np.float32)
        in_maps.append({
            "xspan": xs, "wqkv": wqkv, "wo": wo, "wft": wft, "acat": acat,
            "bcatx": bcatx, "lsi": lsi, "cost": cost, "sint": sint,
            "padc": padc, "ident": ident, "maskc": maskc, "masks": masks,
            "a2v": a2v, "bq": bq, "onesd": onesd,
        })
    return in_maps


def kernel(**inputs):
    in_maps = prep_in_maps(**inputs)
    if "nc" not in _CACHE:
        _CACHE["nc"] = build()
    nc = _CACHE["nc"]
    res = run_bass_kernel_spmd(nc, in_maps, core_ids=list(range(8)))
    out = np.empty((B, SEQ, D), np.float32)
    for c in range(8):
        b = c // 4
        Q = c % 4
        out[b, Q * 1024:(Q + 1) * 1024] = res.results[c]["out"]
    return out


# version 08
# speedup vs baseline: 1.8241x; 1.0382x over previous
import numpy as np
import ml_dtypes
from contextlib import ExitStack

import concourse.bass as bass
import concourse.mybir as mybir
from concourse.bass_utils import run_bass_kernel_spmd

F32 = mybir.dt.float32
BF16 = mybir.dt.bfloat16
I32 = mybir.dt.int32
AF = mybir.ActivationFunctionType
ALU = mybir.AluOpType
AX = mybir.AxisListType

D = 4096
HD = 128
NKVH = 8
THETA = 10000.0
SM_SCALE = HD ** -0.5
EPS = 1e-5
LORA_SCALE = 1.0
NT_OWN = 8
NT_SPAN = 16
SEQ = 4096
B = 2


class Sched:
    ENGS = ("g", "t", "v", "s", "y")

    def __init__(self):
        self.q = {e: [] for e in self.ENGS}
        self.c = {"PE": 0, "VEC": 0, "SCA": 0}
        self.waited = {}

    def op(self, eng, fn, inc=None, amt=1):
        if inc is not None:
            self.c[inc] = self.c.get(inc, 0) + amt
        self.q[eng].append((fn, inc, amt))
        return self.c[inc] if inc else None

    def wait(self, eng, sem, val):
        if val <= 0:
            return
        key = (eng, sem)
        if self.waited.get(key, 0) >= val:
            return
        self.waited[key] = val
        self.q[eng].append(("W", sem, val))

    def hop(self, eng):
        sem = {"t": "PE", "v": "VEC", "s": "SCA"}[eng]
        self.wait(eng, sem, self.c[sem])

    def barrier(self):
        snap = dict(self.c)
        for e in self.ENGS:
            for sm in snap:
                self.wait(e, sm, snap[sm])

    def replay(self, eng, handle, sems):
        for item in self.q[eng]:
            if item[0] == "W":
                handle.wait_ge(sems[item[1]], item[2])
            else:
                fn, inc, amt = item
                ins = fn(handle)
                if inc is not None:
                    ins.then_inc(sems[inc], amt)


def build(taps=()):
    nc = bass.Bass("TRN2", target_bir_lowering=False)

    xspan = nc.dram_tensor("xspan", [NT_SPAN * 128, D], F32, kind="ExternalInput")
    wqkv = nc.dram_tensor("wqkv", [D, 6144], F32, kind="ExternalInput")
    wo_d = nc.dram_tensor("wo", [D, D], F32, kind="ExternalInput")
    wft = nc.dram_tensor("wft", [D, D], F32, kind="ExternalInput")
    acat_d = nc.dram_tensor("acat", [D, 48], F32, kind="ExternalInput")
    bcat_d = nc.dram_tensor("bcatx", [49, D], F32, kind="ExternalInput")
    lsi_d = nc.dram_tensor("lsi", [1, 4], I32, kind="ExternalInput")
    cost_d = nc.dram_tensor("cost", [128, 2048], F32, kind="ExternalInput")
    sint_d = nc.dram_tensor("sint", [128, 2048], F32, kind="ExternalInput")
    padc_d = nc.dram_tensor("padc", [NT_OWN, 512], F32, kind="ExternalInput")
    ident_d = nc.dram_tensor("ident", [128, 128], F32, kind="ExternalInput")
    maskc_d = nc.dram_tensor("maskc", [128, 512], F32, kind="ExternalInput")
    masks_d = nc.dram_tensor("masks", [128, 512], F32, kind="ExternalInput")
    a2v_d = nc.dram_tensor("a2v", [49, 2], F32, kind="ExternalInput")
    bq_d = nc.dram_tensor("bq", [1, 1], F32, kind="ExternalInput")
    ones_d = nc.dram_tensor("onesd", [1, 1024], F32, kind="ExternalInput")
    out_d = nc.dram_tensor("out", [NT_OWN * 128, D], F32, kind="ExternalOutput")

    qt_scr = nc.dram_tensor("qt_scr", [D, NT_OWN * 128], BF16,
                            kind="ExternalOutput" if "qt" in taps else "Internal")
    x2_scr = nc.dram_tensor("x2_scr", [NT_OWN * 128, D], F32,
                            kind="ExternalOutput" if "x2" in taps else "Internal")

    es = ExitStack()
    sb = lambda n, s, d: es.enter_context(nc.sbuf_tensor(n, s, d))
    ps = lambda n, s, d=F32: es.enter_context(nc.psum_tensor(n, s, d))

    bigT = sb("bigT", [128, 32768], BF16)
    kTb = sb("kTb", [128, 16384], BF16)
    Vb = sb("Vb", [128, 16384], BF16)
    arA0 = sb("arA0", [128, 4096], F32)
    arA1 = sb("arA1", [128, 4096], F32)
    arB = sb("arB", [128, 4096], F32)
    wb = [sb("wb0", [128, 4096], BF16), sb("wb1", [128, 4096], BF16)]
    xnb = sb("xnb", [128, D], BF16)
    identb = sb("identb", [128, 128], BF16)
    maskcb = sb("maskcb", [128, 512], BF16)
    masksb = sb("masksb", [128, 512], BF16)
    onescol = sb("onescol", [128, 1], BF16)
    onesrow = sb("onesrow", [1, 128], F32)
    onesrowb = sb("onesrowb", [1, 128], BF16)
    stats = sb("stats", [128, 16], F32)
    lsib = sb("lsib", [1, 4], I32)
    lsif = sb("lsif", [1, 8], F32)
    rsc = sb("rsc", [1, 8], F32)
    a2vb = sb("a2vb", [49, 2], F32)
    w49 = sb("w49", [49, 1], F32)
    eqv = sb("eqv", [49, 1], F32)
    bqb = sb("bqb", [1, 1], F32)
    lnh = sb("lnh", [128, 224], F32) if "lnh" in taps else None

    rsb = arA1[0:1, 1024:1536]
    rsb2 = arA1[0:1, 1536:2048]
    xnbf = xnb.bitcast(F32)
    padtile = xnbf[0:1, 0:1024]
    arA0b = arA0.bitcast(BF16)
    arA1b = arA1.bitcast(BF16)
    arBb = arB.bitcast(BF16)
    xst = [arA0[:, 0:4096], arA1[:, 0:4096]]
    qrope = arA0b[:, 6144:8192]
    qstage = arA1b[:, 6144:8192]
    costb = arB[:, 0:2048]
    sintb = arB[:, 2048:4096]
    ptb = [arA0b[:, 0:512], arA0b[:, 512:1024], arA0b[:, 1024:1536]]
    recipb = [arA1[:, 0:512], arA1[:, 512:1024]]
    qtt = [arBb[:, 0:4096], arBb[:, 4096:8192]]
    colst = [arB[:, 0:2048], arB[:, 2048:4096]]
    ostage = [arA0[:, 0:2048], arA1[:, 0:2048]]
    acat_off = 4096
    zTs_off = 6144
    zbuf_off = 7168

    pA = ps("pA", [128, 2048])
    pB = ps("pB", [128, 2048])
    pAb = pA.bitcast(BF16)
    pBb = pB.bitcast(BF16)

    S = Sched()
    free = {}

    def need(eng, tok):
        if tok in free:
            sem, val = free[tok]
            S.wait(eng, sem, val)

    def release(tok, sem, val):
        free[tok] = (sem, val)

    def dma(eng_q, out_ap, in_ap, sem):
        return S.op(eng_q, lambda E, o=out_ap, i=in_ap: E.dma_start(o, i), inc=sem, amt=16)

    def tap(name, region, shape, dt):
        if name not in taps:
            return
        od = nc.dram_tensor("tap_" + name, shape, dt, kind="ExternalOutput")
        S.barrier()
        dma("y", od[:, :], region, "TP")
        S.barrier()

    gd_ident = dma("g", identb[:, :], ident_d[:, :], "CST")
    dma("g", maskcb[:, :], maskc_d[:, :], "CST")
    dma("g", masksb[:, :], masks_d[:, :], "CST")
    dma("g", costb, cost_d[:, :], "CST")
    dma("g", sintb, sint_d[:, :], "CST")
    dma("g", lsib[:, :], lsi_d[:, :], "CST")
    dma("g", a2vb[:, :], a2v_d[:, :], "CST")
    gd_setup = dma("g", bqb[:, :], bq_d[:, :], "CST")

    S.op("v", lambda E: E.memset(onescol[:, :], 1.0), inc="VEC")
    S.op("v", lambda E: E.memset(onesrow[:, :], 1.0), inc="VEC")
    S.op("v", lambda E: E.memset(onesrowb[:, :], 1.0), inc="VEC")
    S.wait("v", "CST", gd_setup)
    S.op("v", lambda E: E.tensor_copy(lsif[:, 0:4], lsib[:, :]), inc="VEC"); S.hop("v")
    S.op("v", lambda E: E.tensor_scalar(lsif[:, 4:8], lsif[:, 0:4], bqb[0:1, 0:1], None, ALU.is_gt), inc="VEC"); S.hop("v")
    c0, c1, c2, c3 = (lsif[:, 4 + i:5 + i] for i in range(4))
    S.op("v", lambda E: E.tensor_tensor(rsc[:, 0:1], c0, c1, ALU.mult), inc="VEC"); S.hop("v")
    S.op("v", lambda E: E.tensor_tensor(rsc[:, 1:2], rsc[:, 0:1], c1, ALU.add), inc="VEC"); S.hop("v")
    S.op("v", lambda E: E.tensor_scalar(rsc[:, 2:3], rsc[:, 1:2], -1.0, 2.0, ALU.mult, ALU.add), inc="VEC"); S.hop("v")
    S.op("v", lambda E: E.tensor_scalar(rsc[:, 3:4], c3, -1.0, 4.0, ALU.mult, ALU.add), inc="VEC"); S.hop("v")
    S.op("v", lambda E: E.tensor_tensor(rsc[:, 4:5], rsc[:, 2:3], rsc[:, 3:4], ALU.subtract), inc="VEC"); S.hop("v")
    S.op("v", lambda E: E.tensor_tensor(rsc[:, 5:6], c2, rsc[:, 4:5], ALU.mult), inc="VEC"); S.hop("v")
    vr = S.op("v", lambda E: E.tensor_tensor(rsc[:, 6:7], rsc[:, 5:6], rsc[:, 3:4], ALU.add), inc="VEC"); S.hop("v")
    S.wait("t", "VEC", vr)
    p_r = S.op("t", lambda E: E.matmul(pB[0:49, 0:1], onesrow[0:1, 0:49], rsc[:, 6:7],
                                       start=True, stop=True, skip_group_check=True), inc="PE")
    S.wait("v", "PE", p_r)
    S.op("v", lambda E: E.tensor_tensor(eqv[:, :], pB[0:49, 0:1], a2vb[:, 0:1], ALU.is_equal), inc="VEC"); S.hop("v")
    S.op("v", lambda E: E.tensor_scalar(eqv[:, :], eqv[:, :], LORA_SCALE, None, ALU.mult), inc="VEC"); S.hop("v")
    vw49 = S.op("v", lambda E: E.tensor_tensor(w49[:, :], eqv[:, :], a2vb[:, 1:2], ALU.add), inc="VEC"); S.hop("v")

    xnT_base = lambda dm, t: (dm * 8 + t) * 128
    ln_state = {"ctr": 0, "last_tr": {0: 0, 1: 0}}

    def ln_tile(par, ti, xsem, xval, xnb2=None):
        st = stats[:, (ln_state["ctr"] % 2) * 8:(ln_state["ctr"] % 2) * 8 + 8]
        ln_state["ctr"] += 1
        xcur = xnb if (xnb2 is None or par == 0) else xnb2
        xb = xst[par]
        S.wait("v", xsem, xval)
        S.wait("s", xsem, xval)
        S.op("v", lambda E, xb=xb, st=st: E.reduce_sum(st[:, 0:1], xb, AX.X), inc="VEC")
        S.wait("s", "PE", ln_state["last_tr"][par])
        sq = S.op("s", lambda E, xb=xb, st=st, xcur=xcur: E.activation(
            xcur[:, :], xb, AF.Square, accum_out=st[:, 1:2]), inc="SCA"); S.hop("s")
        S.op("s", lambda E: E.nop(), inc="SCA")
        S.wait("v", "SCA", S.c["SCA"])
        S.op("v", lambda E, st=st: E.tensor_scalar(st[:, 2:3], st[:, 1:2], 1.0 / D, EPS, ALU.mult, ALU.add), inc="VEC")
        S.op("v", lambda E, st=st: E.tensor_scalar(st[:, 3:4], st[:, 0:1], 1.0 / D, None, ALU.mult), inc="VEC"); S.hop("v")
        S.op("v", lambda E, st=st: E.tensor_tensor(st[:, 4:5], st[:, 3:4], st[:, 3:4], ALU.mult), inc="VEC"); S.hop("v")
        vv = S.op("v", lambda E, st=st: E.tensor_tensor(st[:, 5:6], st[:, 2:3], st[:, 4:5], ALU.subtract), inc="VEC"); S.hop("v")
        S.wait("s", "VEC", vv)
        sd = S.op("s", lambda E, st=st: E.activation(st[:, 6:7], st[:, 5:6], AF.Sqrt), inc="SCA")
        S.wait("v", "SCA", sd)
        S.op("v", lambda E, st=st: E.reciprocal(st[:, 7:8], st[:, 6:7]), inc="VEC"); S.hop("v")
        nrm = S.op("v", lambda E, xb=xb, st=st, xcur=xcur: E.tensor_scalar(
            xcur[:, :], xb, st[:, 3:4], st[:, 7:8], ALU.subtract, ALU.mult), inc="VEC")
        if lnh is not None:
            k8 = ln_state["ctr"] - 1
            S.op("v", lambda E, st=st, k8=k8: E.tensor_copy(lnh[:, k8 * 8:k8 * 8 + 8], st), inc="VEC")
        release(("xst", par), "VEC", nrm)
        for gblk in range(4):
            tb = pAb[:, (gblk % 2) * 1024:(gblk % 2) * 1024 + 1024]
            need("t", ("trbank", gblk % 2))
            S.wait("t", "VEC", nrm)
            pe_last = None
            for j in range(8):
                dm = gblk * 8 + j
                pe_last = S.op("t", lambda E, dm=dm, tb=tb, j=j, xcur=xcur: E.transpose(
                    tb[:, j * 128:(j + 1) * 128], xcur[:, dm * 128:(dm + 1) * 128], identb[:, :]), inc="PE")
            base0 = xnT_base(gblk * 8, ti)
            dst = bass.AP(bigT, base0, [[32768, 128], [1024, 8], [1, 128]])
            if gblk == 0:
                S.wait("v", "PE", pe_last)
                ev = S.op("v", lambda E, tb=tb, dst=dst: E.tensor_copy(dst, tb[:, 0:1024]), inc="VEC")
                release(("trbank", gblk % 2), "VEC", ev)
            else:
                S.wait("s", "PE", pe_last)
                ev = S.op("s", lambda E, tb=tb, dst=dst: E.copy(dst, tb[:, 0:1024]), inc="SCA")
                release(("trbank", gblk % 2), "SCA", ev)
        ln_state["last_tr"][par] = S.c["PE"]

    WPAIR_A = ((wb[0], 0), (wb[1], 0), "WB", "wb")
    WPAIR_WO = ((kTb, 0), (kTb, 8192), "WC", "wbC")
    WPAIR_FF = ((Vb, 0), (Vb, 4096), "WC", "wbC")

    def stream_weights(dram, col0, pair=WPAIR_A):
        b0, b1, semp, tok = pair
        vals = []
        for wh in range(2):
            buf, boff = (b0, b1)[wh]
            need("g", (tok, wh))
            src = bass.AP(dram, (wh * 2048) * dram.shape[1] + col0,
                          [[dram.shape[1], 128], [128 * dram.shape[1], 16], [1, 256]])
            dst = bass.AP(buf, boff, [[buf.shape[1], 128], [256, 16], [1, 256]])
            vals.append(dma("g", dst, src, "%s%d" % (semp, wh)))
        return vals

    def proj_og(par, gd_after, no_stop=False, pair=WPAIR_A):
        b0, b1, semp, tok = pair
        pX = pA if par == 0 else pB
        pe_last = None
        for wh in range(2):
            buf, boff = (b0, b1)[wh]
            S.wait("t", "%s%d" % (semp, wh), gd_after[wh])
            for dm in range(16):
                dmg = wh * 16 + dm
                for t in range(8):
                    st_ = (wh == 0 and dm == 0 and t % 2 == 0)
                    sp_ = (wh == 1 and dm == 15) and not no_stop
                    pe_last = S.op("t", lambda E, pX=pX, t=t, dmg=dmg, dm=dm, st_=st_, sp_=sp_, buf=buf, boff=boff:
                                   E.matmul(pX[:, t * 256:(t + 1) * 256],
                                            bigT[:, xnT_base(dmg, t):xnT_base(dmg, t) + 128],
                                            buf[:, boff + dm * 256:boff + (dm + 1) * 256],
                                            start=st_, stop=sp_, skip_group_check=True), inc="PE")
            release((tok, wh), "PE", pe_last)
        return pe_last

    xnb2_h = Vb[:, 8192:12288]
    S.wait("t", "CST", gd_ident)
    for h in range(2):
        for i in range(8):
            lv = 8 * h + i
            par = i % 2
            need("y", ("xst", par))
            gd_x = dma("y", xst[par], xspan[lv * 128:(lv + 1) * 128, :], "XST%d" % par)
            ln_tile(par, i, "XST%d" % par, gd_x, xnb2=xnb2_h)
        S.barrier()
        if h == 0:
            tap("xnT0", bigT[:, :], [128, 32768], BF16)

        og_list = list(range(24)) if h == 1 else list(range(16, 24))
        pend = None

        def emit_tr_evac(og, par, kind, rope_done):
            S.wait("t", "VEC", rope_done)
            pXb_ = pAb if par == 0 else pBb
            pe_tr = None
            for t in range(8):
                for hh in range(2):
                    j = t * 2 + hh
                    pe_tr = S.op("t", lambda E, j=j, pXb_=pXb_: E.transpose(
                        pXb_[:, j * 128:(j + 1) * 128],
                        qrope[:, j * 128:(j + 1) * 128], identb[:, :]), inc="PE")
            release(("qrope",), "PE", pe_tr)
            S.wait("v", "PE", pe_tr)
            if kind == "q":
                need("v", ("qstage",))
                ev = S.op("v", lambda E, pXb_=pXb_: E.tensor_copy(qstage[:, 0:2048], pXb_[:, 0:2048]), inc="VEC")
                release(("pogpar", par), "VEC", ev)
                S.wait("y", "VEC", ev)
                sy = None
                for hh in range(2):
                    dst = bass.AP(qt_scr, (og * 256 + hh * 128) * 1024, [[1024, 128], [128, 8], [1, 128]])
                    src = bass.AP(arA1.bitcast(BF16), 6144 + hh * 128, [[8192, 128], [256, 8], [1, 128]])
                    sy = dma("y", dst, src, "QS")
                release(("qstage",), "QS", sy)
            else:
                ko = og - 16
                dst = bass.AP(kTb, (ko * 2) * 2048 + (8 * h) * 128,
                              [[16384, 128], [2048, 2], [128, 8], [1, 128]])
                srcb = bass.AP(pXb_, 0, [[4096, 128], [128, 2], [256, 8], [1, 128]])
                ev = S.op("v", lambda E, dst=dst, srcb=srcb: E.tensor_copy(dst, srcb), inc="VEC")
                release(("pogpar", par), "VEC", ev)

        for og in og_list:
            par = og % 2
            gd_after = stream_weights(wqkv, og * 256)
            need("t", ("pogpar", par))
            pe_mm = proj_og(par, gd_after)
            if pend is not None:
                emit_tr_evac(*pend)
                pend = None
            pX = pA if par == 0 else pB
            kind = "q" if og < 16 else ("k" if og < 20 else "v")
            if kind in ("q", "k"):
                S.wait("v", "PE", pe_mm)
                need("v", ("qrope",))
                q1 = bass.AP(pX, 0, [[2048, 128], [256, 8], [128, 2], [1, 64]])
                q2 = bass.AP(pX, 64, [[2048, 128], [256, 8], [128, 2], [1, 64]])
                cosv = bass.AP(arB, (8 * h) * 128, [[4096, 128], [128, 8], [64, 2], [1, 64]])
                sinv = bass.AP(arB, 2048 + (8 * h) * 128, [[4096, 128], [128, 8], [64, 2], [1, 64]])
                t1v = bass.AP(arA0, 2048, [[4096, 128], [128, 8], [64, 2], [1, 64]])
                t2v = bass.AP(arA1, 2048, [[4096, 128], [128, 8], [64, 2], [1, 64]])
                qr1 = bass.AP(arA0.bitcast(BF16), 6144, [[8192, 128], [256, 8], [128, 2], [1, 64]])
                qr2 = bass.AP(arA0.bitcast(BF16), 6144 + 64, [[8192, 128], [256, 8], [128, 2], [1, 64]])
                S.op("v", lambda E, q1=q1, cosv=cosv, t1v=t1v: E.tensor_tensor(t1v, q1, cosv, ALU.mult), inc="VEC")
                S.op("v", lambda E, q2=q2, sinv=sinv, t2v=t2v: E.tensor_tensor(t2v, q2, sinv, ALU.mult), inc="VEC")
                S.op("v", lambda E, t1v=t1v, t2v=t2v, qr1=qr1: E.tensor_tensor(qr1, t1v, t2v, ALU.subtract), inc="VEC")
                S.op("v", lambda E, q1=q1, sinv=sinv, t1v=t1v: E.tensor_tensor(t1v, q1, sinv, ALU.mult), inc="VEC")
                S.op("v", lambda E, q2=q2, cosv=cosv, t2v=t2v: E.tensor_tensor(t2v, q2, cosv, ALU.mult), inc="VEC")
                rope_done = S.op("v", lambda E, t1v=t1v, t2v=t2v, qr2=qr2: E.tensor_tensor(qr2, t1v, t2v, ALU.add), inc="VEC")
                pend = (og, par, kind, rope_done)
            else:
                vo = og - 20
                S.wait("s", "PE", pe_mm)
                dst = bass.AP(Vb, (8 * h) * 1024 + vo * 256, [[16384, 128], [1024, 8], [1, 256]])
                src = bass.AP(pX, 0, [[2048, 128], [256, 8], [1, 256]])
                ev = S.op("s", lambda E, dst=dst, src=src: E.copy(dst, src), inc="SCA")
                release(("pogpar", par), "SCA", ev)
        if pend is not None:
            emit_tr_evac(*pend)
            pend = None
        S.barrier()

    tap("kT", kTb[:, :], [128, 16384], BF16)
    tap("V", Vb[:, :], [128, 16384], BF16)

    ptb4 = [arA0b[:, i * 512:(i + 1) * 512] for i in range(4)]
    psT3 = [pA[:, 0:512], pA[:, 512:1024], pA[:, 1024:1536]]
    psums = pA[0:1, 1536:2048]
    pav3 = [pB[:, 0:512], pB[:, 512:1024], pB[:, 1024:1536]]
    pbc = pB[:, 1536:2048]
    rsbP = [arA1[0:1, 1024:1536], arA1[0:1, 1536:2048]]
    rsbFP = [arA1[0:1, 2048:2560], arA1[0:1, 2560:3072]]
    rsb2P = [arA1b[0:1, 6144:6656], arA1b[0:1, 6656:7168]]
    LAG = 3

    jobs = []
    for t in range(NT_OWN):
        for hk in range(NKVH):
            for c in range(9):
                jobs.append((t, hk, c))

    exp_of = {}
    gate_of = {}
    qt_of = {}
    pd_of = {}
    sub_of = {}
    recip_of = {}
    bc_of = {}
    cp_of = {}
    mult_of = {}
    pair_pe_last = {}

    def att_loads(t):
        qsem = "QT%d" % (t % 2)
        if t == 0:
            S.wait("y", "QS", S.c.get("QS", 0))
        need("y", ("qtt", t % 2))
        src = bass.AP(qt_scr, t * 128, [[1024, 128], [128 * 1024, 32], [1, 128]])
        dst = bass.AP(arB.bitcast(BF16), (t % 2) * 4096, [[8192, 128], [128, 32], [1, 128]])
        qt_of[t] = dma("y", dst, src, qsem)
        need("y", ("padt", t % 2))
        pd_of[t] = dma("y", xnbf[0:1, (t % 2) * 512:(t % 2) * 512 + 512], padc_d[t:t + 1, :], qsem)

    att_loads(0)
    att_loads(1)
    gd_wo = {0: stream_weights(wo_d, 0)}

    def emit_mm(idx):
        t, hk, c = jobs[idx]
        bank = idx % 3
        buf = idx % 4
        lvk = t + c
        if idx >= 3:
            S.wait("t", "SCA", exp_of[idx - 3])
        if c == 0 and hk == 0:
            S.wait("t", "QT%d" % (t % 2), pd_of[t])
        qt_rhs = qtt[t % 2][:, (4 * hk) * 128:(4 * hk) * 128 + 512]
        mm = S.op("t", lambda E, bank=bank, hk=hk, lvk=lvk, qt_rhs=qt_rhs:
                  E.matmul(psT3[bank], kTb[:, hk * 2048 + lvk * 128:hk * 2048 + lvk * 128 + 128],
                           qt_rhs, start=True, stop=True, skip_group_check=True), inc="PE")
        S.wait("s", "PE", mm)
        need("s", ("ptb", buf))
        ex = S.op("s", lambda E, buf=buf, bank=bank: E.activation(
            ptb4[buf], psT3[bank], AF.Exp, scale=SM_SCALE), inc="SCA")
        exp_of[idx] = ex
        gate_of[idx] = ("SCA", ex)
        if c == 0 or c == 8:
            S.wait("v", "SCA", ex)
            mk = masksb if c == 0 else maskcb
            mv = S.op("v", lambda E, buf=buf, mk=mk: E.tensor_tensor(ptb4[buf], ptb4[buf], mk[:, :], ALU.mult), inc="VEC")
            gate_of[idx] = ("VEC", mv)

    def emit_acc(idx):
        t, hk, c = jobs[idx]
        buf = idx % 4
        p = t * NKVH + hk
        lvk = t + c
        gs, gv = gate_of[idx]
        S.wait("t", gs, gv)
        if c != 0:
            if c == 1:
                if p >= 1:
                    S.wait("t", "VEC", sub_of[p - 1])
                s0 = S.op("t", lambda E, buf4=(idx - 1) % 4: E.matmul(
                    psums, onescol[:, :], ptb4[buf4], start=True, stop=False,
                    skip_group_check=True), inc="PE")
                release(("ptb", (idx - 1) % 4), "PE", s0)
            S.op("t", lambda E, c=c, buf=buf: E.matmul(
                psums, onescol[:, :], ptb4[buf], start=False, stop=(c == 8),
                skip_group_check=True), inc="PE")
        if c == 0 and p >= 3:
            S.wait("t", "VEC", mult_of[p - 3])
        av = S.op("t", lambda E, c=c, p=p, hk=hk, lvk=lvk, buf=buf: E.matmul(
            pav3[p % 3], Vb[:, lvk * 1024 + hk * 128:lvk * 1024 + hk * 128 + 128],
            ptb4[buf], start=(c == 0), stop=(c == 8), skip_group_check=True), inc="PE")
        if c != 0:
            release(("ptb", buf), "PE", av)
        if c == 8:
            pair_pe_last[p] = S.c["PE"]
            emit_f1(p)
            if p >= 1:
                emit_f2(p - 1)

    def emit_f1(p):
        t, hk = p // NKVH, p % NKVH
        qsem = "QT%d" % (t % 2)
        S.wait("v", "PE", pair_pe_last[p])
        S.wait("v", qsem, pd_of[t])
        if p >= 2:
            S.wait("v", "PE", bc_of[p - 2])
        sub_of[p] = S.op("v", lambda E, p=p, t=t: E.tensor_tensor(
            rsbP[p % 2], psums, xnbf[0:1, (t % 2) * 512:(t % 2) * 512 + 512], ALU.subtract), inc="VEC")
        S.hop("v")
        S.op("v", lambda E, p=p: E.reciprocal(rsbFP[p % 2], rsbP[p % 2]), inc="VEC")
        S.hop("v")
        recip_of[p] = S.op("v", lambda E, p=p: E.tensor_copy(rsb2P[p % 2], rsbFP[p % 2]), inc="VEC")
        S.hop("v")
        if hk == NKVH - 1:
            release(("padt", t % 2), "VEC", sub_of[p])
            release(("qtt", t % 2), "PE", pair_pe_last[p])
            if t + 2 < NT_OWN:
                att_loads(t + 2)

    def emit_f2(p):
        t, hk = p // NKVH, p % NKVH
        S.wait("t", "VEC", recip_of[p])
        if p >= 1:
            S.wait("t", "VEC", cp_of[p - 1])
        bc_of[p] = S.op("t", lambda E, p=p: E.matmul(
            pbc, onesrowb[0:1, :], rsb2P[p % 2], start=True, stop=True,
            skip_group_check=True), inc="PE")
        S.wait("v", "PE", bc_of[p])
        cp_of[p] = S.op("v", lambda E, p=p: E.tensor_copy(recipb[p % 2], pbc), inc="VEC")
        dstb = bass.AP(bigT, (4 * hk * 8 + t) * 128, [[32768, 128], [1024, 4], [1, 128]])
        mult_of[p] = S.op("v", lambda E, p=p, dstb=dstb: E.tensor_tensor(
            dstb, pav3[p % 3], recipb[p % 2], ALU.mult), inc="VEC")

    for idx in range(len(jobs)):
        emit_mm(idx)
        if idx >= LAG:
            emit_acc(idx - LAG)
    for idx in range(len(jobs) - LAG, len(jobs)):
        emit_acc(idx)
    emit_f2(NT_OWN * NKVH - 1)
    S.barrier()
    tap("attnT", bigT[:, :], [128, 32768], BF16)

    gd_wo[1] = stream_weights(wo_d, 256, WPAIR_WO)
    gd_ff = {}
    for og in range(16):
        par = og % 2
        wpair = WPAIR_A if par == 0 else WPAIR_WO
        need("y", ("colst", par))
        csrc = bass.AP(xspan, (8 * 128) * D + og * 256, [[D, 128], [128 * D, 8], [1, 256]])
        cdst = bass.AP(arB, par * 2048, [[4096, 128], [256, 8], [1, 256]])
        sy_col = dma("y", cdst, csrc, "CL%d" % par)
        need("t", ("pogpar", par))
        pe_mm = proj_og(par, gd_wo[og], pair=wpair)
        if og + 2 < 16:
            gd_wo[og + 2] = stream_weights(wo_d, (og + 2) * 256, wpair)
        elif og == 14:
            gd_ff[0] = stream_weights(wft, 0)
        elif og == 15:
            gd_ff[1] = stream_weights(wft, 256, WPAIR_FF)
        pX = pA if par == 0 else pB
        S.wait("v", "PE", pe_mm)
        S.wait("v", "CL%d" % par, sy_col)
        need("v", ("ostage", par))
        ev = S.op("v", lambda E, pX=pX, par=par: E.tensor_tensor(ostage[par], pX[:, 0:2048], colst[par], ALU.add), inc="VEC")
        release(("pogpar", par), "VEC", ev)
        release(("colst", par), "VEC", ev)
        S.wait("y", "VEC", ev)
        dst = bass.AP(x2_scr, og * 256, [[D, 128], [128 * D, 8], [1, 256]])
        src = bass.AP(arA0 if par == 0 else arA1, 0, [[4096, 128], [256, 8], [1, 256]])
        syx2 = dma("y", dst, src, "OS%d" % par)
        release(("ostage", par), "OS%d" % par, syx2)
    S.barrier()

    adst = bass.AP(kTb, acat_off, [[16384, 128], [48, 32], [1, 48]])
    asrc = bass.AP(acat_d, 0, [[48, 128], [128 * 48, 32], [1, 48]])
    gd_ac = dma("g", adst, asrc, "CST")
    gd_bc = dma("g", kTb[0:49, 0:4096], bcat_d[:, :], "CST")
    gd_ac = gd_bc = dma("g", kTb[48:49, zTs_off:zTs_off + 1024], ones_d[:, :], "CST")

    S.wait("y", "OS0", S.c.get("OS0", 0))
    S.wait("y", "OS1", S.c.get("OS1", 0))
    xnb2_2 = kTb[:, 8192:12288]
    pzT = pB[0:48, 1536:1664]
    for t in range(NT_OWN):
        par = t % 2
        need("y", ("xst", par))
        gd_x = dma("y", xst[par], x2_scr[t * 128:(t + 1) * 128, :], "XST%d" % par)
        ln_tile(par, t, "XST%d" % par, gd_x, xnb2=xnb2_2)
        S.wait("t", "CST", gd_ac)
        need("t", ("pz",))
        S.wait("t", "SCA", S.c["SCA"])
        S.wait("t", "VEC", S.c["VEC"])
        pe_z = None
        for dm in range(32):
            pe_z = S.op("t", lambda E, dm=dm, t=t: E.matmul(
                pzT, kTb[:, acat_off + dm * 48:acat_off + dm * 48 + 48],
                bigT[:, xnT_base(dm, t):xnT_base(dm, t) + 128],
                start=(dm == 0), stop=(dm == 31), skip_group_check=True), inc="PE")
        S.wait("v", "PE", pe_z)
        S.wait("v", "VEC", vw49)
        vz = S.op("v", lambda E, t=t: E.tensor_scalar(
            kTb[0:48, zTs_off + t * 128:zTs_off + (t + 1) * 128], pzT, w49[0:48, 0:1], None, ALU.mult), inc="VEC")
        release(("pz",), "VEC", vz)
    ln2_vec = S.c["VEC"]
    S.barrier()
    tap("xn2T", bigT[:, :], [128, 32768], BF16)
    tap("zTs", kTb[0:49, zTs_off:zTs_off + 1024], [49, 1024], BF16)
    tap("w49", w49[:, :], [49, 1], F32)
    tap("lsig", lsif[:, :], [1, 8], F32)
    tap("rscg", rsc[:, :], [1, 8], F32)
    tap("eqvg", eqv[:, :], [49, 1], F32)

    for og in range(16):
        par = og % 2
        wpair = WPAIR_A if par == 0 else WPAIR_FF
        need("y", ("colst", par))
        csrc = bass.AP(x2_scr, og * 256, [[D, 128], [128 * D, 8], [1, 256]])
        cdst = bass.AP(arB, par * 2048, [[4096, 128], [256, 8], [1, 256]])
        sy_col = dma("y", cdst, csrc, "CL%d" % par)
        need("t", ("pogpar", par))
        pe_mm = proj_og(par, gd_ff[og], no_stop=True, pair=wpair)
        if og + 2 < 16:
            gd_ff[og + 2] = stream_weights(wft, (og + 2) * 256, wpair)
        pX = pA if par == 0 else pB
        pe_l = None
        for t in range(8):
            pe_l = S.op("t", lambda E, t=t, og=og, pX=pX: E.matmul(
                pX[:, t * 256:(t + 1) * 256], kTb[0:49, zTs_off + t * 128:zTs_off + (t + 1) * 128],
                kTb[0:49, og * 256:(og + 1) * 256],
                start=False, stop=True, skip_group_check=True), inc="PE")
        S.wait("v", "PE", pe_l)
        S.wait("v", "CL%d" % par, sy_col)
        need("v", ("ostage", par))
        ev = S.op("v", lambda E, pX=pX, par=par: E.tensor_tensor(ostage[par], pX[:, 0:2048], colst[par], ALU.add), inc="VEC")
        release(("pogpar", par), "VEC", ev)
        release(("colst", par), "VEC", ev)
        S.wait("y", "VEC", ev)
        dst = bass.AP(out_d, og * 256, [[D, 128], [128 * D, 8], [1, 256]])
        src = bass.AP(arA0 if par == 0 else arA1, 0, [[4096, 128], [256, 8], [1, 256]])
        syo = dma("y", dst, src, "OS%d" % par)
        release(("ostage", par), "OS%d" % par, syo)

    if lnh is not None:
        tap("lnh", lnh[:, :], [128, 224], F32)
    S.wait("y", "OS0", S.c.get("OS0", 0))
    S.wait("y", "OS1", S.c.get("OS1", 0))

    with ExitStack() as emit_es:
        sems = {n: emit_es.enter_context(nc.semaphore(n)) for n in sorted(S.c)}
        with nc.Block() as block:
            @block.gpsimd
            def _(g):
                S.replay("g", g, sems)

            @block.tensor
            def _(t):
                S.replay("t", t, sems)

            @block.vector
            def _(v):
                S.replay("v", v, sems)

            @block.scalar
            def _(s):
                S.replay("s", s, sems)

            @block.sync
            def _(y):
                S.replay("y", y, sems)
    es.close()
    return nc


_CACHE = {}


def _host_constants():
    if "consts" in _CACHE:
        return _CACHE["consts"]
    ident = np.eye(128, dtype=np.float32)
    cc = np.arange(128)[:, None]
    rr = np.arange(128)[None, :]
    causal = (cc <= rr).astype(np.float32)
    strict = (cc > rr).astype(np.float32)
    maskc = np.tile(causal, (1, 4)).astype(np.float32)
    masks = np.tile(strict, (1, 4)).astype(np.float32)
    a2v = np.zeros((49, 2), dtype=np.float32)
    a2v[0:48, 0] = (np.arange(48) // 16 + 2).astype(np.float32)
    a2v[48, 0] = -1.0
    a2v[48, 1] = 1.0
    _CACHE["consts"] = (ident, maskc, masks, a2v)
    return _CACHE["consts"]


def prep_in_maps(L0_attn_x, L0_attn_wq, L0_attn_wk, L0_attn_wv, L0_attn_wo,
                 L0_ffn_weight, L0_ffn_bias, L0_ffn_lora_A, L0_ffn_lora_B,
                 L0_ffn_lora_start_indices):
    x = np.ascontiguousarray(np.asarray(L0_attn_x, dtype=np.float32))
    wqkv = np.ascontiguousarray(np.concatenate(
        [np.asarray(L0_attn_wq, np.float32), np.asarray(L0_attn_wk, np.float32),
         np.asarray(L0_attn_wv, np.float32)], axis=1))
    wo = np.ascontiguousarray(np.asarray(L0_attn_wo, np.float32))
    wft = np.ascontiguousarray(np.asarray(L0_ffn_weight, np.float32).T)
    acat = np.ascontiguousarray(np.asarray(L0_ffn_lora_A, np.float32).transpose(2, 0, 1).reshape(D, 48))
    bcatx = np.ascontiguousarray(np.concatenate(
        [np.asarray(L0_ffn_lora_B, np.float32).transpose(0, 2, 1).reshape(48, D),
         np.asarray(L0_ffn_bias, np.float32)[None, :]], axis=0))
    lsi = np.ascontiguousarray(np.asarray(L0_ffn_lora_start_indices).astype(np.int32).reshape(1, 4))
    ident, maskc, masks, a2v = _host_constants()

    inv_freq = THETA ** (-np.arange(64, dtype=np.float64) / 64.0)

    in_maps = []
    for c in range(8):
        b = c // 4
        Q = c % 4
        if Q == 0:
            xs = np.concatenate([np.zeros((1024, D), np.float32), x[b, 0:1024]], axis=0)
        else:
            xs = np.ascontiguousarray(x[b, (Q - 1) * 1024:(Q + 1) * 1024])
        span0 = Q * 1024 - 1024
        pos = np.maximum(span0 + np.arange(2048), 0).astype(np.float64)
        ang = pos[:, None] * inv_freq[None, :]
        cosv = np.cos(ang).astype(np.float32).reshape(16, 128, 64)
        sinv = np.sin(ang).astype(np.float32).reshape(16, 128, 64)
        cost = np.zeros((128, 16, 2, 64), np.float32)
        sint = np.zeros((128, 16, 2, 64), np.float32)
        for hh in range(2):
            cost[:, :, hh, :] = cosv.transpose(1, 0, 2)
            sint[:, :, hh, :] = sinv.transpose(1, 0, 2)
        cost = np.ascontiguousarray(cost.reshape(128, 2048))
        sint = np.ascontiguousarray(sint.reshape(128, 2048))
        padc = np.zeros((8, 512), np.float32)
        if Q == 0:
            rq = np.arange(128, dtype=np.float32)
            for t in range(8):
                padc[t] = np.tile((127.0 - rq) + 128.0 * (7 - t), 4)
        bq = np.full((1, 1), float(b), np.float32)
        onesd = np.ones((1, 1024), np.float32)
        in_maps.append({
            "xspan": xs, "wqkv": wqkv, "wo": wo, "wft": wft, "acat": acat,
            "bcatx": bcatx, "lsi": lsi, "cost": cost, "sint": sint,
            "padc": padc, "ident": ident, "maskc": maskc, "masks": masks,
            "a2v": a2v, "bq": bq, "onesd": onesd,
        })
    return in_maps


def kernel(**inputs):
    in_maps = prep_in_maps(**inputs)
    if "nc" not in _CACHE:
        _CACHE["nc"] = build()
    nc = _CACHE["nc"]
    res = run_bass_kernel_spmd(nc, in_maps, core_ids=list(range(8)))
    out = np.empty((B, SEQ, D), np.float32)
    for c in range(8):
        b = c // 4
        Q = c % 4
        out[b, Q * 1024:(Q + 1) * 1024] = res.results[c]["out"]
    return out
